# version 03
# baseline (speedup 1.0000x reference)
import numpy as np
from contextlib import ExitStack
import concourse.bass as bass
import concourse.mybir as mybir
from concourse.bass_utils import run_bass_kernel_spmd

F32 = mybir.dt.float32
BF16 = mybir.dt.bfloat16
AF = mybir.ActivationFunctionType
ALU = mybir.AluOpType
AX = mybir.AxisListType.X

NCORE = 8
NT = 2048
NS = 4
SW = 512
NTT = 16
DC = 8
IC = 4
E = 16
HC = 8
D = 1024
I = 512

PHASES = ("ffn1", "attn", "ffn2")
DBG = False
LAST_RES = None

CONST_VALS = [1e-5, -0.5, 1.0 / 32, -1.0 / 1024, 1.0 / 1024, 0.5, -1e30, 0.0, 1.0, -1.0, 1e30]

ENGS = ("tensor", "vector", "scalar", "gpsimd", "sync")


class IR:

    def __init__(self):
        self.ops = []

    def op(self, eng, fn, inc=None, n=1, waits=(), hard=()):
        o = {"eng": eng, "fn": fn, "inc": inc, "n": n, "val": None,
             "waits": [h for h in waits if h is not None],
             "hard": [h for h in hard if h is not None]}
        self.ops.append(o)
        return o

    def finalize(self, block, sems):
        cnt = {}
        for o in self.ops:
            if o["inc"]:
                cnt[o["inc"]] = cnt.get(o["inc"], 0) + o["n"]
                o["val"] = cnt[o["inc"]]
        for o in self.ops:
            for h in o["waits"]:
                assert h["inc"] is not None, "wait on op without inc"
                assert h["val"] <= cnt[h["inc"]], "wait exceeds final count"
        idx = {e: [] for e in ENGS}
        for o in self.ops:
            idx[o["eng"]].append(o)
        pos = {e: 0 for e in ENGS}
        sim = {}
        progressed = True
        while progressed:
            progressed = False
            for e in ENGS:
                while pos[e] < len(idx[e]):
                    o = idx[e][pos[e]]
                    if any(sim.get(h["inc"], 0) < h["val"] for h in o["waits"]
                           if h["eng"] != e):
                        break
                    if any(sim.get(h["inc"], 0) < h["val"] for h in o.get("hard", ())
                           if h["eng"] != e):
                        break
                    if o["inc"]:
                        sim[o["inc"]] = sim.get(o["inc"], 0) + o["n"]
                    pos[e] += 1
                    progressed = True
        stuck = {e: (pos[e], len(idx[e])) for e in ENGS if pos[e] < len(idx[e])}
        if stuck:
            for e, (p, n) in stuck.items():
                o = idx[e][p]
                bad = [(h["inc"], h["val"], sim.get(h["inc"], 0)) for h in o["waits"]
                       if h["eng"] != e and sim.get(h["inc"], 0) < h["val"]]
                print(f"STUCK {e}@{p}/{n}: needs {bad}")
            raise AssertionError(f"DEADLOCK in schedule: {stuck}")
        for eng in ENGS:
            if not idx[eng]:
                continue

            def body(ectx, eng=eng):
                protect = eng in ("vector", "scalar")
                guaranteed = {}
                for o in idx[eng]:
                    for h in list(o["hard"]) + list(o["waits"]):
                        s, v = h["inc"], h["val"]
                        if h["eng"] == eng and not protect:
                            continue
                        if guaranteed.get(s, 0) >= v:
                            continue
                        ectx.wait_ge(sems[s], v)
                        guaranteed[s] = v
                    if o["fn"] is not None:
                        instr = o["fn"](ectx)
                        if o["inc"]:
                            instr.then_inc(sems[o["inc"]], o["n"])
            getattr(block, eng)(body)


def build_program():
    nc = bass.Bass("TRN2", target_bir_lowering=False, debug=False)

    def din(name, shape):
        return nc.dram_tensor(name, shape, F32, kind="ExternalInput").ap()

    x_in = din("x_in", [D, NT])
    wqT_d = din("wqT", [D, D])
    wkT_d = din("wkT", [D, D])
    wvT_d = din("wvT", [D, D])
    bq_pp = din("bq_pp", [128, DC])
    bk_pp = din("bk_pp", [128, DC])
    bv_row = din("bv_row", [1, D])
    gwT_d = din("gwT", [D, E])
    wg_all = din("wg_all", [E, D, I])
    wu_all = din("wu_all", [E, D, I])
    wd_all = din("wd_all", [E, I, D])
    wsg_d = din("wsg", [D, D])
    wsu_d = din("wsu", [D, D])
    wsd_d = din("wsd", [D, D])
    ones_col = din("ones_col", [128, 1])
    ones_row = din("ones_row", [1, 128])
    oh_d = din("oh", [E, E * 128])
    idn_d = din("idn", [128, 128])
    cvals_d = din("cvals", [128, 16])
    y_out = nc.dram_tensor("y", [D, NT], F32, kind="ExternalOutput").ap()
    k_in = nc.dram_tensor("k_in", [D, NT], BF16).ap()
    v_in = nc.dram_tensor("v_in", [NT, D], BF16).ap()
    k_out = nc.dram_tensor("k_out", [2 * D, NT], BF16).ap()
    v_out = nc.dram_tensor("v_out", [2 * NT, D], BF16).ap()
    q_hbm = nc.dram_tensor("q_hbm", [D, NT], BF16).ap()
    xp_d = nc.dram_tensor("xp", [D, NT], F32).ap()
    x2_d = nc.dram_tensor("x2", [D, NT], F32).ap()
    if DBG:
        dbg_crow = nc.dram_tensor("dbg_crow", [E, NT], BF16, kind="ExternalOutput").ap()
        dbg_cb = nc.dram_tensor("dbg_cb", [128, 2 * SW], F32, kind="ExternalOutput").ap()
        dbg_xl = nc.dram_tensor("dbg_xl", [128, NT], BF16, kind="ExternalOutput").ap()
        dbg_act = nc.dram_tensor("dbg_act", [128, IC * NT], BF16, kind="ExternalOutput").ap()
        dbg_es = nc.dram_tensor("dbg_es", [6, 128, 4 * E], F32, kind="ExternalOutput").ap()
        dbg_m = nc.dram_tensor("dbg_m", [3, 128, 1], F32, kind="ExternalOutput").ap()
    if DBG == 3:
        dbg_q = nc.dram_tensor("dbg_q", [D, NT], BF16, kind="ExternalOutput").ap()
        dbg_k = nc.dram_tensor("dbg_k", [D, NT], BF16, kind="ExternalOutput").ap()
        dbg_v = nc.dram_tensor("dbg_v", [NT, D], BF16, kind="ExternalOutput").ap()
        dbg_pt = nc.dram_tensor("dbg_pt", [128, 32 * SW], BF16, kind="ExternalOutput").ap()
        dbg_l = nc.dram_tensor("dbg_l", [1, SW], F32, kind="ExternalOutput").ap()

    ir = IR()
    RG = [[0, 1], [2, 3], [4, 5], [6, 7]]

    def OP(eng, fn, inc=None, n=1, waits=(), hard=()):
        return ir.op(eng, fn, inc=inc, n=n, waits=waits, hard=hard)


    def r3i(ap, a, b):
        return ap.rearrange("(a p) b -> p a b", p=128)

    def r3o(ap, a):
        return ap.rearrange("p (a b) -> p a b", a=a)

    with ExitStack() as top:
        sems = {s: top.enter_context(nc.semaphore(s))
                for s in ["ts", "vs", "ss", "gd", "yd", "cc"]}

        ones_bf = top.enter_context(nc.sbuf_tensor("ones_bf", [1, 128], BF16))
        ones_f = top.enter_context(nc.sbuf_tensor("ones_f", [1, 128], F32))
        onec_bf = top.enter_context(nc.sbuf_tensor("onec_bf", [128, 1], BF16))
        oh_bf = top.enter_context(nc.sbuf_tensor("oh_bf", [E, E * 128], BF16))
        idn_bf = top.enter_context(nc.sbuf_tensor("idn_bf", [128, 128], BF16))
        gw_sb = top.enter_context(nc.sbuf_tensor("gw_sb", [128, DC * E], BF16))
        bq_sb = top.enter_context(nc.sbuf_tensor("bq_sb", [128, DC], F32))
        bk_sb = top.enter_context(nc.sbuf_tensor("bk_sb", [128, DC], F32))
        bvr_bf = top.enter_context(nc.sbuf_tensor("bvr_bf", [1, D], BF16))
        cv_f = top.enter_context(nc.sbuf_tensor("cv_f", [128, 16], F32))
        cv_b = top.enter_context(nc.sbuf_tensor("cv_b", [128, 16], BF16))
        for i, v in enumerate(CONST_VALS):
            nc.const_aps.register(F32, v, cv_f[:, i:i + 1])
            nc.const_aps.register(BF16, v, cv_b[:, i:i + 1])

        block = top.enter_context(nc.Block())

        cl = []
        for dst, src in [(ones_bf, ones_row), (ones_f, ones_row), (onec_bf, ones_col),
                         (oh_bf, oh_d), (idn_bf, idn_d), (bq_sb, bq_pp), (bk_sb, bk_pp),
                         (bvr_bf, bv_row), (cv_f, cvals_d), (cv_b, cvals_d)]:
            cl.append(OP("gpsimd", lambda g, d=dst, s=src: g.dma_start(out=d[:, :], in_=s[:, :]),
                         inc="gd", n=16))
        cl.append(OP("gpsimd", lambda g: g.dma_start(
            out=r3o(gw_sb[:, :], DC), in_=r3i(gwT_d, DC, E)),
            inc="gd", n=16))
        consts = cl[-1]

        def ffn_scope_tensors(ctx, tag):
            t = {}
            t["x"] = ctx.enter_context(nc.sbuf_tensor(f"x_{tag}", [128, DC * NT], F32))
            t["xl"] = ctx.enter_context(nc.sbuf_tensor(f"xl_{tag}", [128, DC * NT], BF16))
            t["abuf"] = ctx.enter_context(nc.sbuf_tensor(f"abf_{tag}", [128, IC * NT], BF16))
            t["wg"] = ctx.enter_context(nc.sbuf_tensor(f"wg_{tag}", [128, 2 * DC * I], BF16))
            t["wu"] = ctx.enter_context(nc.sbuf_tensor(f"wu_{tag}", [128, 2 * DC * I], BF16))
            t["wd"] = ctx.enter_context(nc.sbuf_tensor(f"wd_{tag}", [128, 2 * IC * D], BF16))
            t["ws1"] = ctx.enter_context(nc.sbuf_tensor(f"ws1_{tag}", [128, 2 * DC * 128], BF16))
            t["ws2"] = ctx.enter_context(nc.sbuf_tensor(f"ws2_{tag}", [128, 2 * DC * 128], BF16))
            t["ws3"] = ctx.enter_context(nc.sbuf_tensor(f"ws3_{tag}", [128, 2 * DC * 128], BF16))
            t["crow"] = ctx.enter_context(nc.sbuf_tensor(f"crow_{tag}", [E, NT], BF16))
            t["cb"] = ctx.enter_context(nc.sbuf_tensor(f"cb_{tag}", [128, 2 * SW], F32))
            t["rows"] = t["cb"]
            t["aba"] = ctx.enter_context(nc.sbuf_tensor(f"aba_{tag}", [128, SW], F32))
            t["abb"] = ctx.enter_context(nc.sbuf_tensor(f"abb_{tag}", [128, SW], F32))
            t["silu"] = ctx.enter_context(nc.sbuf_tensor(f"sil_{tag}", [128, 2 * SW], F32))
            t["tts"] = ctx.enter_context(nc.sbuf_tensor(f"tts_{tag}", [128, 2 * SW], F32))
            t["xbf"] = ctx.enter_context(nc.sbuf_tensor(f"xbf_{tag}", [128, 2 * SW], BF16))
            t["x2bf"] = t["xbf"]
            t["rbf"] = ctx.enter_context(nc.sbuf_tensor(f"rbf_{tag}", [1, 3 * SW], BF16))
            t["esc"] = ctx.enter_context(nc.sbuf_tensor(f"esc_{tag}", [128, 4 * E], F32))
            t["cols"] = ctx.enter_context(nc.sbuf_tensor(f"col_{tag}", [128, 8], F32))
            for nm in ("nm1s", "nuac", "nui1", "m1s", "m2s"):
                t[nm] = ctx.enter_context(nc.sbuf_tensor(f"{nm}_{tag}", [128, 1], F32))
            t["ctm"] = ctx.enter_context(nc.sbuf_tensor(f"ctm_{tag}", [128, E], BF16))
            t["qst"] = ctx.enter_context(nc.sbuf_tensor(f"qst_{tag}", [128, 2 * SW], BF16))
            t["pg"] = [ctx.enter_context(nc.psum_tensor(f"pg{i}_{tag}", [128, SW], F32)) for i in (0, 1)]
            t["pu"] = [ctx.enter_context(nc.psum_tensor(f"pu{i}_{tag}", [128, SW], F32)) for i in (0, 1)]
            t["pd"] = [ctx.enter_context(nc.psum_tensor(f"pd{i}_{tag}", [128, SW], F32)) for i in (0, 1)]
            t["pb"] = ctx.enter_context(nc.psum_tensor(f"pb_{tag}", [128, SW], F32))
            t["pbt"] = ctx.enter_context(nc.psum_tensor(f"pbt_{tag}", [E, 128], BF16))
            return t

        def emit_ln(t, xready, trk):
            xl_done = {}
            bufp = {"c0": None, "c1": None, "q0": None, "q1": None}
            pbu = trk.get("pb_user")
            pd0u = trk.get("pd0_user")
            for s in range(NS):
                sum_mm = None
                for kc in range(DC):
                    p2 = kc % 2
                    cp = OP("vector", lambda v, kc=kc, s=s, p2=p2: v.tensor_copy(
                        t["xbf"][:, p2 * SW:(p2 + 1) * SW],
                        t["x"][:, kc * NT + s * SW: kc * NT + (s + 1) * SW]),
                        inc="vs", waits=xready + [bufp[f"c{p2}"]])
                    sum_mm = OP("tensor", lambda tn, kc=kc, p2=p2: tn.matmul(
                        t["pb"][0:1, :], onec_bf[:, :], t["xbf"][:, p2 * SW:(p2 + 1) * SW],
                        start=(kc == 0), stop=(kc == DC - 1)),
                        inc="ts", waits=[cp, consts] + ([pbu] if kc == 0 else []))
                    bufp[f"c{p2}"] = sum_mm
                sq_mm = None
                bufp["q0"], bufp["q1"] = bufp["c0"], bufp["c1"]
                for kc in range(DC):
                    p2 = kc % 2
                    sq = OP("scalar", lambda sc, kc=kc, s=s, p2=p2: sc.square(
                        t["x2bf"][:, p2 * SW:(p2 + 1) * SW],
                        t["x"][:, kc * NT + s * SW: kc * NT + (s + 1) * SW]),
                        inc="ss", waits=xready + [bufp[f"q{p2}"]])
                    sq_mm = OP("tensor", lambda tn, kc=kc, p2=p2: tn.matmul(
                        t["pd"][0][0:1, :], onec_bf[:, :], t["x2bf"][:, p2 * SW:(p2 + 1) * SW],
                        start=(kc == 0), stop=(kc == DC - 1)),
                        inc="ts", waits=[sq] + ([pd0u] if kc == 0 else []))
                    bufp[f"q{p2}"] = sq_mm
                bufp["c0"], bufp["c1"] = bufp["q0"], bufp["q1"]
                R, RB = t["rows"], t["rbf"]
                nm = OP("vector", lambda v: v.tensor_scalar_mul(
                    RB[0:1, 0:SW], t["pb"][0:1, :], -1.0 / D), inc="vs", waits=[sum_mm])
                m2r = OP("vector", lambda v: v.tensor_tensor(
                    R[0:1, 0:SW], RB[0:1, 0:SW], RB[0:1, 0:SW], ALU.mult), inc="vs", waits=[nm])
                vr = OP("vector", lambda v: v.scalar_tensor_tensor(
                    R[0:1, SW:2 * SW], t["pd"][0][0:1, :], 1.0 / D, R[0:1, 0:SW],
                    ALU.mult, ALU.subtract), inc="vs", waits=[sq_mm, m2r])
                lnv = OP("scalar", lambda sc: sc.activation(
                    R[0:1, 0:SW], R[0:1, SW:2 * SW], AF.Ln, bias=1e-5),
                    inc="ss", waits=[vr, m2r])
                ar = OP("scalar", lambda sc: sc.activation(
                    RB[0:1, SW:2 * SW], R[0:1, 0:SW], AF.Exp, scale=-0.5),
                    inc="ss", waits=[lnv])
                br = OP("vector", lambda v: v.tensor_tensor(
                    RB[0:1, 2 * SW:3 * SW], RB[0:1, SW:2 * SW], RB[0:1, 0:SW], ALU.mult),
                    inc="vs", waits=[ar, nm])
                ba_mm = OP("tensor", lambda tn: tn.matmul(
                    t["pb"][:, :], ones_bf[:, :], RB[0:1, SW:2 * SW], start=True, stop=True),
                    inc="ts", waits=[ar, nm])
                bb_mm = OP("tensor", lambda tn: tn.matmul(
                    t["pd"][0][:, :], ones_bf[:, :], RB[0:1, 2 * SW:3 * SW], start=True, stop=True),
                    inc="ts", waits=[br, vr])
                ca = OP("scalar", lambda sc: sc.copy(t["aba"][:, :], t["pb"][:, :]),
                        inc="ss", waits=[ba_mm])
                cb_ = OP("scalar", lambda sc: sc.copy(t["abb"][:, :], t["pd"][0][:, :]),
                         inc="ss", waits=[bb_mm])
                pbu, pd0u = ca, cb_
                for kc in range(DC):
                    p2 = kc % 2
                    t1 = OP("vector", lambda v, kc=kc, s=s, p2=p2: v.tensor_tensor(
                        t["tts"][:, p2 * SW:(p2 + 1) * SW],
                        t["x"][:, kc * NT + s * SW: kc * NT + (s + 1) * SW],
                        t["aba"][:, :], ALU.mult), inc="vs", waits=[ca])
                    t2 = OP("vector", lambda v, kc=kc, s=s, p2=p2: v.tensor_tensor(
                        t["xl"][:, kc * NT + s * SW: kc * NT + (s + 1) * SW],
                        t["tts"][:, p2 * SW:(p2 + 1) * SW], t["abb"][:, :], ALU.add),
                        inc="vs", waits=[cb_, t1])
                    xl_done[(kc, s)] = t2
            xl_done["all"] = t2
            trk["pb_user"] = pbu
            trk["pd0_user"] = pd0u
            return xl_done

        def emit_logits(t, xl_done, trk):
            crow_done = {}
            prev_exp = trk.get("pb_user")
            prev_cpy = None
            prev_tp = None
            for s in range(NS):
                for j in range(4):
                    tt = s * 4 + j
                    lmm = None
                    for kc in range(DC):
                        lmm = OP("tensor", lambda tn, kc=kc, tt=tt: tn.matmul(
                            t["pb"][:, 0:E],
                            t["xl"][:, kc * NT + tt * 128: kc * NT + (tt + 1) * 128],
                            gw_sb[:, kc * E:(kc + 1) * E],
                            start=(kc == 0), stop=(kc == DC - 1)),
                            inc="ts",
                            waits=([xl_done[(DC - 1, s)], consts, prev_exp] if kc == 0 else []))
                    ES = t["esc"]
                    nm1 = OP("vector", lambda v: v.reduce_max(
                        t["nm1s"][:, :], t["pb"][:, 0:E], AX, negate=True), inc="vs", waits=[lmm])
                    ex = OP("scalar", lambda sc: sc.activation(
                        ES[:, 0:E], t["pb"][:, 0:E], AF.Exp, bias=t["nm1s"][:, :],
                        accum_out=t["nuac"][:, :]), inc="ss", waits=[nm1])
                    prev_exp = ex
                    nui = OP("vector", lambda v: v.reciprocal(t["nui1"][:, :], t["nuac"][:, :]),
                             inc="vs", waits=[ex])
                    m1e = OP("vector", lambda v: v.reduce_max(t["m1s"][:, :], ES[:, 0:E], AX),
                             inc="vs", waits=[ex])
                    if DBG == 2 and tt == 0 and t.get("is_A"):
                        dd = OP("sync", lambda sy: sy.dma_start(out=dbg_m[0], in_=t["m1s"][:, :]),
                                inc="yd", n=16, waits=[m1e])
                        m1e = dd
                    d1 = OP("vector", lambda v: v.tensor_scalar(
                        ES[:, E:2 * E], ES[:, 0:E], t["m1s"][:, :], 0.0, ALU.subtract, ALU.add),
                        inc="vs", hard=[m1e])
                    if DBG == 2 and tt == 0 and t.get("is_A"):
                        dd = OP("sync", lambda sy: sy.dma_start(out=dbg_es[0], in_=ES[:, :]),
                                inc="yd", n=16, waits=[d1])
                        d1 = dd
                    tnm = OP("vector", lambda v: v.tensor_scalar(
                        ES[:, 2 * E:3 * E], ES[:, E:2 * E], -1e30, 1.0, ALU.mult, ALU.min),
                        inc="vs", waits=[d1])
                    if DBG == 2 and tt == 0 and t.get("is_A"):
                        dd = OP("sync", lambda sy: sy.dma_start(out=dbg_es[1], in_=ES[:, :]),
                                inc="yd", n=16, waits=[tnm])
                        tnm = dd
                    e2 = OP("vector", lambda v: v.tensor_tensor(
                        ES[:, E:2 * E], ES[:, 0:E], ES[:, 2 * E:3 * E], ALU.mult),
                        inc="vs", waits=[tnm])
                    if DBG == 2 and tt == 0 and t.get("is_A"):
                        dd = OP("sync", lambda sy: sy.dma_start(out=dbg_es[2], in_=ES[:, :]),
                                inc="yd", n=16, waits=[e2])
                        e2 = dd
                    m2e = OP("vector", lambda v: v.reduce_max(t["m2s"][:, :], ES[:, E:2 * E], AX),
                             inc="vs", waits=[e2])
                    if DBG == 2 and tt == 0 and t.get("is_A"):
                        dd = OP("sync", lambda sy: sy.dma_start(out=dbg_m[1], in_=t["m2s"][:, :]),
                                inc="yd", n=16, waits=[m2e])
                        m2e = dd
                    d2 = OP("vector", lambda v: v.tensor_scalar(
                        ES[:, 2 * E:3 * E], ES[:, 0:E], t["m2s"][:, :], 0.0, ALU.subtract, ALU.add),
                        inc="vs", hard=[m2e])
                    if DBG == 2 and tt == 0 and t.get("is_A"):
                        dd = OP("sync", lambda sy: sy.dma_start(out=dbg_es[3], in_=ES[:, :]),
                                inc="yd", n=16, waits=[d2])
                        d2 = dd
                    u0 = OP("vector", lambda v: v.tensor_scalar(
                        ES[:, E:2 * E], ES[:, 2 * E:3 * E], 1e30, 1.0, ALU.mult, ALU.add),
                        inc="vs", waits=[d2])
                    if DBG == 2 and tt == 0 and t.get("is_A"):
                        dd = OP("sync", lambda sy: sy.dma_start(out=dbg_es[4], in_=ES[:, :]),
                                inc="yd", n=16, waits=[u0])
                        u0 = dd
                    msk = OP("vector", lambda v: v.tensor_scalar(
                        ES[:, 2 * E:3 * E], ES[:, E:2 * E], 0.0, 1.0, ALU.max, ALU.min),
                        inc="vs", waits=[u0])
                    if DBG == 2 and tt == 0 and t.get("is_A"):
                        dm1 = OP("sync", lambda sy: sy.dma_start(out=dbg_m[0], in_=t["m1s"][:, :]),
                                 inc="yd", n=16, waits=[m1e])
                        dm2 = OP("sync", lambda sy: sy.dma_start(out=dbg_m[1], in_=t["m2s"][:, :]),
                                 inc="yd", n=16, waits=[m2e])
                        dm3 = OP("sync", lambda sy: sy.dma_start(out=dbg_m[2], in_=t["nui1"][:, :]),
                                 inc="yd", n=16, waits=[nui])
                        des = OP("sync", lambda sy: sy.dma_start(out=dbg_es[5], in_=ES[:, :]),
                                 inc="yd", n=16, waits=[msk])
                        msk = des
                    w_ = OP("vector", lambda v: v.tensor_tensor(
                        ES[:, 3 * E:4 * E], ES[:, 0:E], ES[:, 2 * E:3 * E], ALU.mult),
                        inc="vs", waits=[msk])
                    ctm = OP("vector", lambda v: v.tensor_scalar(
                        t["ctm"][:, :], ES[:, 3 * E:4 * E], t["nui1"][:, :], 0.5,
                        ALU.mult, ALU.mult), inc="vs", waits=[w_, prev_tp], hard=[nui])
                    tp = OP("tensor", lambda tn: tn.transpose(
                        t["pbt"][0:E, :], t["ctm"][:, :], idn_bf[:, :]),
                        inc="ts", waits=[ctm, prev_cpy, consts])
                    prev_tp = tp
                    cpy = OP("vector", lambda v, tt=tt: v.tensor_copy(
                        t["crow"][:, tt * 128:(tt + 1) * 128], t["pbt"][0:E, :]),
                        inc="vs", waits=[tp])
                    prev_cpy = cpy
                    crow_done[tt] = cpy
            for s in range(NS):
                crow_done[("s", s)] = crow_done[s * 4 + 3]
            trk["pb_user"] = prev_exp
            return crow_done

        def emit_ffn(t, tag, xready, trk):
            xl_done = emit_ln(t, xready, trk)
            crow_done = emit_logits(t, xl_done, trk)

            wg_ld, wu_ld, wd_ld = {}, {}, {}
            gu_user = {0: None, 1: None}
            wd_user = None
            wd_user_par = {0: None, 1: None}
            xadd = {}
            cb_ev = {}
            pg_free = {0: trk.get("pg0_user"), 1: trk.get("pg1_user")}
            pu_free = {0: trk.get("pu0_user"), 1: trk.get("pu1_user")}
            pd_free = {0: trk.get("pd0_user"), 1: trk.get("pd1_user")}
            act_done = {}
            prev_cb = trk.get("pb_user")

            def load_expert(e, dep_g):
                wg_ld[e] = OP("gpsimd", lambda g, e=e: g.dma_start(
                    out=r3o(t["wg"][:, (e % 2) * DC * I:(e % 2 + 1) * DC * I], DC),
                    in_=r3i(wg_all[e], DC, I)),
                    inc="gd", n=16, waits=[dep_g])
                wu_ld[e] = OP("gpsimd", lambda g, e=e: g.dma_start(
                    out=r3o(t["wu"][:, (e % 2) * DC * I:(e % 2 + 1) * DC * I], DC),
                    in_=r3i(wu_all[e], DC, I)),
                    inc="gd", n=16, waits=[dep_g])

            def load_wd(e, dep_d):
                wd_ld[e] = OP("gpsimd", lambda g, e=e: g.dma_start(
                    out=r3o(t["wd"][:, (e % 2) * IC * D:(e % 2 + 1) * IC * D], IC),
                    in_=r3i(wd_all[e], IC, D)),
                    inc="gd", n=16, waits=[dep_d])

            load_expert(0, trk.get("wgu_user0"))
            load_expert(1, trk.get("wgu_user1"))
            load_wd(0, trk.get("wd_user0"))
            load_wd(1, trk.get("wd_user1"))

            cb_user = {0: None, 1: None}
            silu_reader = {0: trk.get("silu_reader0"), 1: trk.get("silu_reader1")}
            dbg_h = {}
            for e in range(E):
                eb = (e % 2) * DC * I
                for s in range(NS):
                    s2 = s % 2
                    mm = OP("tensor", lambda tn, e=e, s=s: tn.matmul(
                        t["pb"][:, :], oh_bf[:, e * 128:(e + 1) * 128],
                        t["crow"][:, s * SW:(s + 1) * SW], start=True, stop=True),
                        inc="ts",
                        waits=[crow_done[("s", s)], prev_cb, cb_user[s2]]
                        + ([dbg_h.get("cb"), dbg_h.get("act")] if DBG and tag == "A" else []))
                    ev = OP("scalar", lambda sc, s2=s2: sc.copy(
                        t["cb"][:, s2 * SW:(s2 + 1) * SW], t["pb"][:, :]),
                        inc="ss", waits=[mm])
                    cb_ev[(e, s)] = ev
                    prev_cb = ev
                    for ic in range(IC):
                        p2 = (s * IC + ic) % 2
                        gmm = None
                        for kc in range(DC):
                            gmm = OP("tensor", lambda tn, kc=kc, ic=ic, s=s, p2=p2, eb=eb: tn.matmul(
                                t["pg"][p2][:, :],
                                t["wg"][:, eb + kc * I + ic * 128: eb + kc * I + (ic + 1) * 128],
                                t["xl"][:, kc * NT + s * SW: kc * NT + (s + 1) * SW],
                                start=(kc == 0), stop=(kc == DC - 1)),
                                inc="ts",
                                waits=([wg_ld[e], pg_free[p2], xl_done[(DC - 1, s)]]
                                       if kc == 0 else []))
                        sil = OP("scalar", lambda sc, p2=p2: sc.activation(
                            t["silu"][:, p2 * SW:(p2 + 1) * SW], t["pg"][p2][:, :], AF.Silu),
                            inc="ss", waits=[gmm, silu_reader.get(p2)])
                        pg_free[p2] = sil
                        umm = None
                        for kc in range(DC):
                            umm = OP("tensor", lambda tn, kc=kc, ic=ic, s=s, p2=p2, eb=eb: tn.matmul(
                                t["pu"][p2][:, :],
                                t["wu"][:, eb + kc * I + ic * 128: eb + kc * I + (ic + 1) * 128],
                                t["xl"][:, kc * NT + s * SW: kc * NT + (s + 1) * SW],
                                start=(kc == 0), stop=(kc == DC - 1)),
                                inc="ts",
                                waits=([wu_ld[e], pu_free[p2]] if kc == 0 else []))
                        tmul = OP("vector", lambda v, p2=p2, s2=s2: v.tensor_tensor(
                            t["tts"][:, p2 * SW:(p2 + 1) * SW], t["pu"][p2][:, :],
                            t["cb"][:, s2 * SW:(s2 + 1) * SW], ALU.mult),
                            inc="vs", waits=[umm, cb_ev[(e, s)]])
                        amul = OP("vector", lambda v, p2=p2, ic=ic, s=s: v.tensor_tensor(
                            t["abuf"][:, ic * NT + s * SW: ic * NT + (s + 1) * SW],
                            t["tts"][:, p2 * SW:(p2 + 1) * SW],
                            t["silu"][:, p2 * SW:(p2 + 1) * SW], ALU.mult),
                            inc="vs", waits=[tmul, sil])
                        act_done[(e, ic, s)] = amul
                        pu_free[p2] = tmul
                        cb_user[s2] = tmul
                        silu_reader[p2] = amul
                if DBG and tag == "A" and e == 0:
                    dbg_h["cb"] = OP("sync", lambda sy: sy.dma_start(
                        out=dbg_cb[:, :], in_=t["cb"][:, :]),
                        inc="yd", n=16, waits=[cb_ev[(0, 0)], cb_ev[(0, 1)]])
                    dbg_h["act"] = OP("sync", lambda sy: sy.dma_start(
                        out=dbg_act[:, :], in_=t["abuf"][:, 0:IC * NT]),
                        inc="yd", n=16,
                        waits=[act_done[(0, i2, s2_)] for i2 in range(IC) for s2_ in range(NS)])
                    dbg_h["crow"] = OP("sync", lambda sy: sy.dma_start(
                        out=dbg_crow[:, :], in_=t["crow"][:, :]),
                        inc="yd", n=16, waits=[crow_done[("s", 3)]])
                    dbg_h["xl"] = OP("sync", lambda sy: sy.dma_start(
                        out=dbg_xl[:, :], in_=t["xl"][:, 0:NT]),
                        inc="yd", n=16, waits=[xl_done["all"]])
                gu_user[e % 2] = umm
                if e + 2 < E:
                    load_expert(e + 2, gu_user[e % 2])
                dmm = None
                eb_d = (e % 2) * IC * D
                for hc in range(HC):
                    for s in range(NS):
                        p2 = (hc * NS + s) % 2
                        for ic in range(IC):
                            dmm = OP("tensor", lambda tn, hc=hc, s=s, ic=ic, p2=p2, eb_d=eb_d: tn.matmul(
                                t["pd"][p2][:, :],
                                t["wd"][:, eb_d + ic * D + hc * 128: eb_d + ic * D + (hc + 1) * 128],
                                t["abuf"][:, ic * NT + s * SW: ic * NT + (s + 1) * SW],
                                start=(ic == 0), stop=(ic == IC - 1)),
                                inc="ts",
                                waits=([wd_ld[e], pd_free[p2], act_done[(e, IC - 1, s)]]
                                       if ic == 0 else []))
                        xa = OP("vector", lambda v, hc=hc, s=s, p2=p2: v.tensor_tensor(
                            t["x"][:, hc * NT + s * SW: hc * NT + (s + 1) * SW],
                            t["pd"][p2][:, :],
                            t["x"][:, hc * NT + s * SW: hc * NT + (s + 1) * SW], ALU.add),
                            inc="vs", waits=[dmm, xadd.get((e - 1, hc, s))])
                        xadd[(e, hc, s)] = xa
                        pd_free[p2] = xa
                wd_user = dmm
                wd_user_par[e % 2] = dmm
                if e + 2 < E:
                    load_wd(e + 2, dmm)

            ws1_ld, ws2_ld, ws3_ld = {}, {}, {}
            ws12_user = {0: trk.get("ws12_user0"), 1: trk.get("ws12_user1")}
            out_h = {}
            prev_half_down = None

            def load_ws12(th, oc):
                dep = ws12_user.get(oc % 2)
                ws1_ld[(th, oc)] = OP("gpsimd", lambda g, oc=oc: g.dma_start(
                    out=r3o(t["ws1"][:, (oc % 2) * DC * 128:(oc % 2 + 1) * DC * 128], DC),
                    in_=r3i(wsg_d[:, oc * 128:(oc + 1) * 128], DC, 128)),
                    inc="gd", n=16, waits=[dep])
                ws2_ld[(th, oc)] = OP("gpsimd", lambda g, oc=oc: g.dma_start(
                    out=r3o(t["ws2"][:, (oc % 2) * DC * 128:(oc % 2 + 1) * DC * 128], DC),
                    in_=r3i(wsu_d[:, oc * 128:(oc + 1) * 128], DC, 128)),
                    inc="gd", n=16, waits=[dep])

            ws3_user = None

            def load_ws3(th, hc):
                ws3_ld[(th, hc)] = OP("gpsimd", lambda g, hc=hc: g.dma_start(
                    out=r3o(t["ws3"][:, (hc % 2) * DC * 128:(hc % 2 + 1) * DC * 128], DC),
                    in_=r3i(wsd_d[:, hc * 128:(hc + 1) * 128], DC, 128)),
                    inc="gd", n=16, waits=[ws3_user])

            for th in range(2):
                sg_done = {}
                load_ws12(th, 0)
                load_ws12(th, 1)
                for oc in range(HC):
                    ob = (oc % 2) * DC * 128
                    for s in (2 * th, 2 * th + 1):
                        p2 = (oc * 2 + s) % 2
                        gmm = None
                        for kc in range(DC):
                            gmm = OP("tensor", lambda tn, oc=oc, s=s, kc=kc, p2=p2, ob=ob: tn.matmul(
                                t["pg"][p2][:, :],
                                t["ws1"][:, ob + kc * 128: ob + (kc + 1) * 128],
                                t["xl"][:, kc * NT + s * SW: kc * NT + (s + 1) * SW],
                                start=(kc == 0), stop=(kc == DC - 1)),
                                inc="ts",
                                waits=([ws1_ld[(th, oc)], pg_free[p2]] if kc == 0 else []))
                        sil = OP("scalar", lambda sc, p2=p2: sc.activation(
                            t["silu"][:, p2 * SW:(p2 + 1) * SW], t["pg"][p2][:, :], AF.Silu),
                            inc="ss", waits=[gmm, silu_reader.get(p2)])
                        pg_free[p2] = sil
                        umm = None
                        for kc in range(DC):
                            umm = OP("tensor", lambda tn, oc=oc, s=s, kc=kc, p2=p2, ob=ob: tn.matmul(
                                t["pu"][p2][:, :],
                                t["ws2"][:, ob + kc * 128: ob + (kc + 1) * 128],
                                t["xl"][:, kc * NT + s * SW: kc * NT + (s + 1) * SW],
                                start=(kc == 0), stop=(kc == DC - 1)),
                                inc="ts",
                                waits=([ws2_ld[(th, oc)], pu_free[p2]] if kc == 0 else []))
                        sg = OP("vector", lambda v, oc=oc, s=s, p2=p2: v.scalar_tensor_tensor(
                            t["abuf"][:, oc * (NT // 2) + (s % 2) * SW: oc * (NT // 2) + (s % 2 + 1) * SW],
                            t["pu"][p2][:, :], 0.5, t["silu"][:, p2 * SW:(p2 + 1) * SW],
                            ALU.mult, ALU.mult),
                            inc="vs", waits=[umm, sil, prev_half_down])
                        sg_done[(oc, s)] = sg
                        pu_free[p2] = sg
                        silu_reader[p2] = sg
                    ws12_user[oc % 2] = umm
                    if oc + 2 < HC:
                        load_ws12(th, oc + 2)

                load_ws3(th, 0)
                load_ws3(th, 1)
                for hc in range(HC):
                    hb = (hc % 2) * DC * 128
                    for s in (2 * th, 2 * th + 1):
                        p2 = (hc * 2 + s) % 2
                        dmm = None
                        for oc in range(DC):
                            dmm = OP("tensor", lambda tn, hc=hc, s=s, oc=oc, p2=p2, hb=hb: tn.matmul(
                                t["pd"][p2][:, :],
                                t["ws3"][:, hb + oc * 128: hb + (oc + 1) * 128],
                                t["abuf"][:, oc * (NT // 2) + (s % 2) * SW: oc * (NT // 2) + (s % 2 + 1) * SW],
                                start=(oc == 0), stop=(oc == DC - 1)),
                                inc="ts",
                                waits=([ws3_ld[(th, hc)], pd_free[p2], sg_done[(DC - 1, s)]]
                                       if oc == 0 else []))
                        xa = OP("vector", lambda v, hc=hc, s=s, p2=p2: v.tensor_tensor(
                            t["x"][:, hc * NT + s * SW: hc * NT + (s + 1) * SW],
                            t["pd"][p2][:, :],
                            t["x"][:, hc * NT + s * SW: hc * NT + (s + 1) * SW], ALU.add),
                            inc="vs", waits=[dmm, xadd[(E - 1, hc, s)]])
                        out_h[(hc, s)] = xa
                        pd_free[p2] = xa
                    ws3_user = dmm
                    if hc + 2 < HC:
                        load_ws3(th, hc + 2)
                prev_half_down = dmm

            trk.update({"silu_reader0": silu_reader[0], "silu_reader1": silu_reader[1]})
            trk.update({"pg0_user": pg_free[0], "pg1_user": pg_free[1],
                        "pu0_user": pu_free[0], "pu1_user": pu_free[1],
                        "pd0_user": pd_free[0], "pd1_user": pd_free[1],
                        "wgu_user0": gu_user[0], "wgu_user1": gu_user[1],
                        "wd_user0": wd_user_par[0], "wd_user1": wd_user_par[1],
                        "ws12_user0": ws12_user[0], "ws12_user1": ws12_user[1],
                        "ws3_user": ws3_user})
            out_h["all"] = xa
            return out_h

        phase = {}

        with ExitStack() as sA:
            t = ffn_scope_tensors(sA, "A")
            t["is_A"] = True
            trk = {}
            x_ld = OP("sync", lambda sy: sy.dma_start(
                out=r3o(t["x"][:, :], DC), in_=r3i(x_in, DC, NT)),
                inc="yd", n=16)
            if "ffn1" in PHASES:
                f1 = emit_ffn(t, "A", [x_ld], trk)
            else:
                f1 = {"all": x_ld}
                for hc in range(HC):
                    for s in range(NS):
                        f1[(hc, s)] = x_ld

            if "attn" in PHASES:
                f1all = [f1[(h2, s2)] for h2 in range(HC) for s2 in range(NS)]
                xl2 = emit_ln(t, f1all, trk)
                bvmm, bvc = [], []
                for dg in (0, 1):
                    mm = OP("tensor", lambda tn, dg=dg: tn.matmul(
                        t["pb"][:, :], ones_bf[:, :], bvr_bf[0:1, dg * SW:(dg + 1) * SW],
                        start=True, stop=True),
                        inc="ts", waits=[consts, xl2["all"], trk.get("pb_user")])
                    ev = OP("scalar", lambda sc, dg=dg: sc.copy(
                        t["cb"][:, dg * SW:(dg + 1) * SW], t["pb"][:, :]),
                        inc="ss", waits=[mm])
                    bvmm.append(mm)
                    bvc.append(ev)
                    trk["pb_user"] = ev

                wq_ld, wk_ld = {}, {}
                qk_user = {0: trk.get("ws12_user0"), 1: trk.get("ws12_user1")}

                def load_qk(oc):
                    dep = qk_user.get(oc % 2)
                    wq_ld[oc] = OP("gpsimd", lambda g, oc=oc: g.dma_start(
                        out=r3o(t["ws1"][:, (oc % 2) * DC * 128:(oc % 2 + 1) * DC * 128], DC),
                        in_=r3i(wqT_d[:, oc * 128:(oc + 1) * 128], DC, 128)),
                        inc="gd", n=16, waits=[dep])
                    wk_ld[oc] = OP("gpsimd", lambda g, oc=oc: g.dma_start(
                        out=r3o(t["ws2"][:, (oc % 2) * DC * 128:(oc % 2 + 1) * DC * 128], DC),
                        in_=r3i(wkT_d[:, oc * 128:(oc + 1) * 128], DC, 128)),
                        inc="gd", n=16, waits=[dep])

                load_qk(0)
                load_qk(1)
                pq_free = {0: trk.get("pg0_user"), 1: trk.get("pg1_user")}
                qst_free = {0: None, 1: None}
                k_stores, v_stores = [], []
                for oc in range(DC):
                    ob = (oc % 2) * DC * 128
                    last_mm = None
                    for which in ("q", "k"):
                        wld = wq_ld if which == "q" else wk_ld
                        wbuf = t["ws1"] if which == "q" else t["ws2"]
                        bias_sb = bq_sb if which == "q" else bk_sb
                        dst = q_hbm if which == "q" else k_in
                        for s in range(NS):
                            p2 = (oc * NS + s) % 2
                            mm = None
                            for kc in range(DC):
                                mm = OP("tensor", lambda tn, wbuf=wbuf, ob=ob, kc=kc, s=s, p2=p2: tn.matmul(
                                    t["pg"][p2][:, :],
                                    wbuf[:, ob + kc * 128: ob + (kc + 1) * 128],
                                    t["xl"][:, kc * NT + s * SW: kc * NT + (s + 1) * SW],
                                    start=(kc == 0), stop=(kc == DC - 1)),
                                    inc="ts",
                                    waits=([wld[oc], pq_free[p2], xl2[(DC - 1, s)]]
                                           if kc == 0 else []))
                            ev = OP("scalar", lambda sc, p2=p2, oc=oc, bias_sb=bias_sb: sc.activation(
                                t["qst"][:, p2 * SW:(p2 + 1) * SW], t["pg"][p2][:, :],
                                AF.Identity, bias=bias_sb[:, oc:oc + 1]),
                                inc="ss", waits=[mm, qst_free[p2], consts])
                            pq_free[p2] = ev
                            st = OP("sync", lambda sy, dst=dst, oc=oc, s=s, p2=p2: sy.dma_start(
                                out=dst[oc * 128:(oc + 1) * 128, s * SW:(s + 1) * SW],
                                in_=t["qst"][:, p2 * SW:(p2 + 1) * SW]),
                                inc="yd", n=16, waits=[ev])
                            qst_free[p2] = st
                            if which == "k":
                                k_stores.append(st)
                            last_mm = mm
                    qk_user[oc % 2] = last_mm
                    if oc + 2 < DC:
                        load_qk(oc + 2)

                wv_ld = {}
                wv_user = {0: trk.get("wgu_user0"), 1: trk.get("wgu_user1")}

                def load_wv(dg):
                    wv_ld[dg] = OP("gpsimd", lambda g, dg=dg: g.dma_start(
                        out=r3o(t["wg"][:, (dg % 2) * DC * I:(dg % 2 + 1) * DC * I], DC),
                        in_=r3i(wvT_d[:, dg * SW:(dg + 1) * SW], DC, SW)),
                        inc="gd", n=16, waits=[wv_user.get(dg % 2)])

                load_wv(0)
                load_wv(1)
                pv_free = {0: trk.get("pu0_user"), 1: trk.get("pu1_user")}
                for tt in range(NTT):
                    for dg in range(2):
                        p2 = (tt * 2 + dg) % 2
                        db = (dg % 2) * DC * I
                        mm = None
                        for kc in range(DC):
                            mm = OP("tensor", lambda tn, tt=tt, dg=dg, kc=kc, p2=p2, db=db: tn.matmul(
                                t["pu"][p2][:, :],
                                t["xl"][:, kc * NT + tt * 128: kc * NT + (tt + 1) * 128],
                                t["wg"][:, db + kc * I: db + kc * I + SW],
                                start=(kc == 0), stop=(kc == DC - 1)),
                                inc="ts",
                                waits=([wv_ld[dg], pv_free[p2], xl2["all"]] if kc == 0 else []))
                        ev = OP("vector", lambda v, p2=p2, dg=dg: v.tensor_tensor(
                            t["qst"][:, p2 * SW:(p2 + 1) * SW], t["pu"][p2][:, :],
                            t["cb"][:, dg * SW:(dg + 1) * SW], ALU.add),
                            inc="vs", waits=[mm, bvc[dg], qst_free[p2]])
                        pv_free[p2] = ev
                        st = OP("sync", lambda sy, tt=tt, dg=dg, p2=p2: sy.dma_start(
                            out=v_in[tt * 128:(tt + 1) * 128, dg * SW:(dg + 1) * SW],
                            in_=t["qst"][:, p2 * SW:(p2 + 1) * SW]),
                            inc="yd", n=16, waits=[ev])
                        qst_free[p2] = st
                        v_stores.append(st)

                phase["a_end"] = mm
                park = OP("sync", lambda sy: sy.dma_start(
                    out=r3i(xp_d, DC, NT), in_=r3o(t["x"][:, :], DC)),
                    inc="yd", n=16, waits=[xl2["all"]] + f1all)
                if DBG == 3:
                    OP("sync", lambda sy: sy.dma_start(out=dbg_q[:, :], in_=q_hbm[:, :]),
                       inc="yd", n=16, waits=[k_stores[-1], v_stores[-1]])
                    OP("sync", lambda sy: sy.dma_start(out=dbg_k[:, :], in_=k_in[:, :]),
                       inc="yd", n=16)
                    OP("sync", lambda sy: sy.dma_start(out=dbg_v[:, :], in_=v_in[:, :]),
                       inc="yd", n=16)
                cck = OP("gpsimd", lambda g: g.collective_compute(
                    "AllGather", ALU.bypass, replica_groups=RG,
                    ins=[k_in.opt()], outs=[k_out.opt()]),
                    inc="cc", waits=[k_stores[-1], phase["a_end"], park])
                ccv = OP("gpsimd", lambda g: g.collective_compute(
                    "AllGather", ALU.bypass, replica_groups=RG,
                    ins=[v_in.opt()], outs=[v_out.opt()]),
                    inc="cc", waits=[v_stores[-1], phase["a_end"], park])
                phase["park"], phase["cck"], phase["ccv"] = park, cck, ccv
            else:
                ystore = OP("sync", lambda sy: sy.dma_start(
                    out=r3i(y_out, DC, NT), in_=r3o(t["x"][:, :], DC)),
                    inc="yd", n=16,
                    waits=[f1[(h2, s2)] for h2 in range(HC) for s2 in range(NS)])
                OP("sync", None, waits=[ystore])

        if "attn" in PHASES:
            with ExitStack() as sB:
                kf = sB.enter_context(nc.sbuf_tensor("kf", [128, DC * 2 * NT], BF16))
                vt = sB.enter_context(nc.sbuf_tensor("vt", [128, 32 * 1024], BF16))
                pt = sB.enter_context(nc.sbuf_tensor("pt", [128, 32 * SW], BF16))
                qsb = sB.enter_context(nc.sbuf_tensor("qsb", [128, 2 * DC * SW], BF16))
                xsts = sB.enter_context(nc.sbuf_tensor("xsts", [128, 4 * SW], F32))
                xnw = sB.enter_context(nc.sbuf_tensor("xnw", [128, 4 * SW], F32))
                lrow = sB.enter_context(nc.sbuf_tensor("lrow", [1, 2 * SW], F32))
                lbc = sB.enter_context(nc.sbuf_tensor("lbc", [128, SW], F32))
                ps = [sB.enter_context(nc.psum_tensor(f"ps{i}", [128, SW], F32)) for i in (0, 1)]
                pav = [sB.enter_context(nc.psum_tensor(f"pv{i}", [128, SW], F32)) for i in (0, 1)]
                pl = sB.enter_context(nc.psum_tensor("pl", [128, SW], F32))

                cck, ccv, park = phase["cck"], phase["ccv"], phase["park"]
                kv_last = None
                for r in range(2):
                    for kc in range(DC):
                        kv_last = OP("sync", lambda sy, r=r, kc=kc: sy.dma_start(
                            out=kf[:, kc * 2 * NT + r * NT: kc * 2 * NT + (r + 1) * NT],
                            in_=k_out[r * D + kc * 128: r * D + (kc + 1) * 128, :]),
                            inc="yd", n=16, waits=[cck, phase["a_end"]])
                for r in range(2):
                    kv_last = OP("sync", lambda sy, r=r: sy.dma_start(
                        out=r3o(vt[:, r * 16 * 1024:(r + 1) * 16 * 1024], 16),
                        in_=r3i(v_out[r * NT:(r + 1) * NT, :], 16, 1024)),
                        inc="yd", n=16, waits=[ccv, phase["a_end"]])

                q_lds = {}

                def load_q(qg, dep):
                    q_lds[qg] = OP("sync", lambda sy, qg=qg: sy.dma_start(
                        out=r3o(qsb[:, (qg % 2) * DC * SW:(qg % 2 + 1) * DC * SW], DC),
                        in_=r3i(q_hbm[:, qg * SW:(qg + 1) * SW], DC, SW)),
                        inc="yd", n=16, waits=[dep])

                load_q(0, phase["a_end"])
                load_q(1, phase["a_end"])
                xst_ld = {}

                def load_xst(n, dep):
                    qg, dc = divmod(n, DC)
                    i4 = n % 4
                    xst_ld[n] = OP("sync", lambda sy, qg=qg, dc=dc, i4=i4: sy.dma_start(
                        out=xsts[:, i4 * SW:(i4 + 1) * SW],
                        in_=xp_d[dc * 128:(dc + 1) * 128, qg * SW:(qg + 1) * SW]),
                        inc="yd", n=16, waits=[park, dep])

                load_xst(0, phase["a_end"])
                load_xst(1, phase["a_end"])

                ps_free = {0: None, 1: None}
                pav_free = {0: None, 1: None}
                xnw_free = {0: None, 1: None, 2: None, 3: None}
                pl_user = None
                lrow_user = {0: None, 1: None}
                lbc_user = None
                last_t1 = None
                last_store = None
                for qg in range(NS):
                    exps = []
                    last_sc_mm = None
                    for kc in range(32):
                        p2 = kc % 2
                        mm = None
                        for dc in range(DC):
                            mm = OP("tensor", lambda tn, qg=qg, kc=kc, dc=dc, p2=p2: tn.matmul(
                                ps[p2][:, :],
                                kf[:, dc * 2 * NT + kc * 128: dc * 2 * NT + (kc + 1) * 128],
                                qsb[:, (qg % 2) * DC * SW + dc * SW:
                                    (qg % 2) * DC * SW + (dc + 1) * SW],
                                start=(dc == 0), stop=(dc == DC - 1)),
                                inc="ts",
                                waits=([kv_last, q_lds[qg], ps_free[p2]] if dc == 0 else []))
                        ex = OP("scalar", lambda sc, kc=kc, p2=p2: sc.activation(
                            pt[:, kc * SW:(kc + 1) * SW], ps[p2][:, :], AF.Exp, scale=1.0 / 32),
                            inc="ss", waits=[mm])
                        ps_free[p2] = ex
                        exps.append(ex)
                        last_sc_mm = mm
                    if qg + 2 < NS:
                        load_q(qg + 2, last_sc_mm)
                    lmm = None
                    for kc in range(32):
                        lmm = OP("tensor", lambda tn, kc=kc: tn.matmul(
                            pl[0:1, :], onec_bf[:, :], pt[:, kc * SW:(kc + 1) * SW],
                            start=(kc == 0), stop=(kc == 31)),
                            inc="ts", waits=([exps[-1], pl_user] if kc == 0 else []))
                    linv = OP("vector", lambda v, qg=qg: v.reciprocal(
                        lrow[0:1, (qg % 2) * SW:(qg % 2 + 1) * SW], pl[0:1, :]),
                        inc="vs", waits=[lmm, lrow_user[qg % 2]])
                    if DBG == 3 and qg == 0:
                        dtp = OP("sync", lambda sy: sy.dma_start(out=dbg_pt[:, :], in_=pt[:, :]),
                                 inc="yd", n=16, waits=[lmm])
                        dtl = OP("sync", lambda sy, qg=qg: sy.dma_start(
                            out=dbg_l[:, :], in_=lrow[0:1, 0:SW]),
                            inc="yd", n=16, waits=[linv])
                        linv = dtl
                        phase[("ptdump", 0)] = dtp
                    lb_mm = OP("tensor", lambda tn, qg=qg: tn.matmul(
                        pl[:, :], ones_f[:, :], lrow[0:1, (qg % 2) * SW:(qg % 2 + 1) * SW],
                        start=True, stop=True), inc="ts", waits=[linv])
                    lrow_user[qg % 2] = lb_mm
                    lb_ev = OP("scalar", lambda sc: sc.copy(lbc[:, :], pl[:, :]),
                               inc="ss", waits=[lb_mm, lbc_user])
                    pl_user = lb_ev
                    for dc in range(DC):
                        n = qg * DC + dc
                        p2 = dc % 2
                        mm = None
                        for kc in range(32):
                            mm = OP("tensor", lambda tn, dc=dc, kc=kc, p2=p2: tn.matmul(
                                pav[p2][:, :],
                                vt[:, kc * 1024 + dc * 128: kc * 1024 + (dc + 1) * 128],
                                pt[:, kc * SW:(kc + 1) * SW],
                                start=(kc == 0), stop=(kc == 31)),
                                inc="ts",
                                waits=([pav_free[p2], lb_ev] if kc == 0 else []))
                        a4, b4 = n % 2, 2 + n % 2
                        t1 = OP("vector", lambda v, p2=p2, a4=a4: v.tensor_tensor(
                            xnw[:, a4 * SW:(a4 + 1) * SW], pav[p2][:, :], lbc[:, :], ALU.mult),
                            inc="vs", waits=[mm, xnw_free[a4], lb_ev])
                        t2 = OP("vector", lambda v, a4=a4, b4=b4, n=n: v.tensor_tensor(
                            xnw[:, b4 * SW:(b4 + 1) * SW], xnw[:, a4 * SW:(a4 + 1) * SW],
                            xsts[:, (n % 4) * SW:((n % 4) + 1) * SW], ALU.add),
                            inc="vs", waits=[t1, xst_ld[n], xnw_free[b4]])
                        pav_free[p2] = t1
                        last_t1 = t1
                        xnw_free[a4] = t2
                        st = OP("sync", lambda sy, qg=qg, dc=dc, b4=b4: sy.dma_start(
                            out=x2_d[dc * 128:(dc + 1) * 128, qg * SW:(qg + 1) * SW],
                            in_=xnw[:, b4 * SW:(b4 + 1) * SW]),
                            inc="yd", n=16, waits=[t2])
                        xnw_free[b4] = st
                        last_store = st
                        if n + 2 < 32:
                            load_xst(n + 2, phase.get(("xst_t2", n - 2)))
                        phase[("xst_t2", n)] = t2
                    lbc_user = last_t1
                phase["x2_done"] = last_store

        if "ffn2" in PHASES:
            with ExitStack() as sC:
                t = ffn_scope_tensors(sC, "C")
                barrier = phase["x2_done"]
                trk2 = {k: barrier for k in (
                    "wgu_user0", "wgu_user1", "wd_user0", "wd_user1",
                    "ws12_user0", "ws12_user1", "pb_user", "pd0_user")}
                x_ld2 = OP("sync", lambda sy: sy.dma_start(
                    out=r3o(t["x"][:, :], DC), in_=r3i(x2_d, DC, NT)),
                    inc="yd", n=16, waits=[phase["x2_done"]])
                f2 = emit_ffn(t, "C", [x_ld2], trk2)
                ystore = OP("sync", lambda sy: sy.dma_start(
                    out=r3i(y_out, DC, NT), in_=r3o(t["x"][:, :], DC)),
                    inc="yd", n=16,
                    waits=[f2[(h2, s2)] for h2 in range(HC) for s2 in range(NS)])
                OP("sync", None, waits=[ystore])
        elif "attn" in PHASES:
            with ExitStack() as sC:
                xt = sC.enter_context(nc.sbuf_tensor("xt_o", [128, DC * NT], F32))
                ld = OP("sync", lambda sy: sy.dma_start(
                    out=r3o(xt[:, :], DC), in_=r3i(x2_d, DC, NT)),
                    inc="yd", n=16, waits=[phase["x2_done"]])
                st = OP("sync", lambda sy: sy.dma_start(
                    out=r3i(y_out, DC, NT), in_=r3o(xt[:, :], DC)),
                    inc="yd", n=16, waits=[ld])
                OP("sync", None, waits=[st])

        ir.finalize(block, sems)
    return nc


def kernel(**inputs):
    f = np.float32
    xfull = np.asarray(inputs["L0_attn_x"], f)
    wq = np.asarray(inputs["L0_attn_wq"], f)
    bq = np.asarray(inputs["L0_attn_bq"], f)
    wk = np.asarray(inputs["L0_attn_wk"], f)
    bk = np.asarray(inputs["L0_attn_bk"], f)
    wv = np.asarray(inputs["L0_attn_wv"], f)
    bv = np.asarray(inputs["L0_attn_bv"], f)
    gw = np.asarray(inputs["L0_ffn_gate_w"], f)

    shared = {
        "wqT": np.ascontiguousarray(wq.T),
        "wkT": np.ascontiguousarray(wk.T),
        "wvT": np.ascontiguousarray(wv.T),
        "bq_pp": np.ascontiguousarray(bq.reshape(DC, 128).T),
        "bk_pp": np.ascontiguousarray(bk.reshape(DC, 128).T),
        "bv_row": np.ascontiguousarray(bv.reshape(1, D)),
        "gwT": np.ascontiguousarray(gw.T),
        "wg_all": np.ascontiguousarray(np.asarray(inputs["L0_ffn_w_gate"], f)),
        "wu_all": np.ascontiguousarray(np.asarray(inputs["L0_ffn_w_up"], f)),
        "wd_all": np.ascontiguousarray(np.asarray(inputs["L0_ffn_w_down"], f)),
        "wsg": np.ascontiguousarray(np.asarray(inputs["L0_ffn_ws_gate"], f)),
        "wsu": np.ascontiguousarray(np.asarray(inputs["L0_ffn_ws_up"], f)),
        "wsd": np.ascontiguousarray(np.asarray(inputs["L0_ffn_ws_down"], f)),
        "ones_col": np.ones((128, 1), f),
        "ones_row": np.ones((1, 128), f),
        "oh": np.kron(np.eye(E, dtype=f), np.ones((1, 128), f)),
        "idn": np.eye(128, dtype=f),
        "cvals": np.tile(np.array(CONST_VALS + [0.0] * (16 - len(CONST_VALS)), f), (128, 1)),
    }
    in_maps = []
    for c in range(NCORE):
        m = dict(shared)
        m["x_in"] = np.ascontiguousarray(xfull[c // 2, (c % 2) * NT:(c % 2 + 1) * NT, :].T)
        in_maps.append(m)

    nc = build_program()
    res = run_bass_kernel_spmd(nc, in_maps, core_ids=list(range(NCORE)))
    global LAST_RES
    LAST_RES = res
    out = np.empty((4, 4096, D), np.float32)
    for c in range(NCORE):
        out[c // 2, (c % 2) * NT:(c % 2 + 1) * NT, :] = res.results[c]["y"].T
    return out


# version 05
# speedup vs baseline: 1.0504x; 1.0504x over previous
import numpy as np
from contextlib import ExitStack
import concourse.bass as bass
import concourse.mybir as mybir
from concourse.bass_utils import run_bass_kernel_spmd

F32 = mybir.dt.float32
BF16 = mybir.dt.bfloat16
AF = mybir.ActivationFunctionType
ALU = mybir.AluOpType
AX = mybir.AxisListType.X

NCORE = 8
NT = 2048
NS = 4
SW = 512
NTT = 16
DC = 8
IC = 4
E = 16
HC = 8
D = 1024
I = 512

PHASES = ("ffn1", "attn", "ffn2")
DBG = False
LAST_RES = None

CONST_VALS = [1e-5, -0.5, 1.0 / 32, -1.0 / 1024, 1.0 / 1024, 0.5, -1e30, 0.0, 1.0, -1.0, 1e30]

ENGS = ("tensor", "vector", "scalar", "gpsimd", "sync")


class IR:

    def __init__(self):
        self.ops = []

    def op(self, eng, fn, inc=None, n=1, waits=(), hard=()):
        o = {"eng": eng, "fn": fn, "inc": inc, "n": n, "val": None,
             "waits": [h for h in waits if h is not None],
             "hard": [h for h in hard if h is not None]}
        self.ops.append(o)
        return o

    def finalize(self, block, sems):
        cnt = {}
        for o in self.ops:
            if o["inc"]:
                cnt[o["inc"]] = cnt.get(o["inc"], 0) + o["n"]
                o["val"] = cnt[o["inc"]]
        for o in self.ops:
            for h in o["waits"]:
                assert h["inc"] is not None, "wait on op without inc"
                assert h["val"] <= cnt[h["inc"]], "wait exceeds final count"
        idx = {e: [] for e in ENGS}
        for o in self.ops:
            idx[o["eng"]].append(o)
        pos = {e: 0 for e in ENGS}
        sim = {}
        progressed = True
        while progressed:
            progressed = False
            for e in ENGS:
                while pos[e] < len(idx[e]):
                    o = idx[e][pos[e]]
                    if any(sim.get(h["inc"], 0) < h["val"] for h in o["waits"]
                           if h["eng"] != e):
                        break
                    if any(sim.get(h["inc"], 0) < h["val"] for h in o.get("hard", ())
                           if h["eng"] != e):
                        break
                    if o["inc"]:
                        sim[o["inc"]] = sim.get(o["inc"], 0) + o["n"]
                    pos[e] += 1
                    progressed = True
        stuck = {e: (pos[e], len(idx[e])) for e in ENGS if pos[e] < len(idx[e])}
        if stuck:
            for e, (p, n) in stuck.items():
                o = idx[e][p]
                bad = [(h["inc"], h["val"], sim.get(h["inc"], 0)) for h in o["waits"]
                       if h["eng"] != e and sim.get(h["inc"], 0) < h["val"]]
                print(f"STUCK {e}@{p}/{n}: needs {bad}")
            raise AssertionError(f"DEADLOCK in schedule: {stuck}")
        for eng in ENGS:
            if not idx[eng]:
                continue

            def body(ectx, eng=eng):
                protect = eng in ("vector", "scalar")
                guaranteed = {}
                for o in idx[eng]:
                    for h in list(o["hard"]) + list(o["waits"]):
                        s, v = h["inc"], h["val"]
                        if h["eng"] == eng and not protect:
                            continue
                        if guaranteed.get(s, 0) >= v:
                            continue
                        ectx.wait_ge(sems[s], v)
                        guaranteed[s] = v
                    if o["fn"] is not None:
                        instr = o["fn"](ectx)
                        if o["inc"]:
                            instr.then_inc(sems[o["inc"]], o["n"])
            getattr(block, eng)(body)


def build_program():
    nc = bass.Bass("TRN2", target_bir_lowering=False, debug=False)

    def din(name, shape):
        return nc.dram_tensor(name, shape, F32, kind="ExternalInput").ap()

    x_in = din("x_in", [D, NT])
    wqT_d = din("wqT", [D, D])
    wkT_d = din("wkT", [D, D])
    wvT_d = din("wvT", [D, D])
    bq_pp = din("bq_pp", [128, DC])
    bk_pp = din("bk_pp", [128, DC])
    bv_row = din("bv_row", [1, D])
    gwT_d = din("gwT", [D, E])
    wg_all = din("wg_all", [E, D, I])
    wu_all = din("wu_all", [E, D, I])
    wd_all = din("wd_all", [E, I, D])
    wsg_d = din("wsg", [D, D])
    wsu_d = din("wsu", [D, D])
    wsd_d = din("wsd", [D, D])
    ones_col = din("ones_col", [128, 1])
    ones_row = din("ones_row", [1, 128])
    oh_d = din("oh", [E, E * 128])
    idn_d = din("idn", [128, 128])
    cvals_d = din("cvals", [128, 16])
    y_out = nc.dram_tensor("y", [D, NT], F32, kind="ExternalOutput").ap()
    kv_in = nc.dram_tensor("kv_in", [2 * NT, D], BF16).ap()
    kv_out = nc.dram_tensor("kv_out", [4 * NT, D], BF16).ap()
    kv3 = kv_in.rearrange("(d h) c -> d h c", h=2)
    kvo4 = kv_out.rearrange("(r d h) c -> r d h c", r=2, h=2)
    q_hbm = nc.dram_tensor("q_hbm", [D, NT], BF16).ap()
    xp_d = nc.dram_tensor("xp", [D, NT], F32).ap()
    x2_d = nc.dram_tensor("x2", [D, NT], F32).ap()
    if DBG:
        dbg_crow = nc.dram_tensor("dbg_crow", [E, NT], BF16, kind="ExternalOutput").ap()
        dbg_cb = nc.dram_tensor("dbg_cb", [128, 2 * SW], F32, kind="ExternalOutput").ap()
        dbg_xl = nc.dram_tensor("dbg_xl", [128, NT], BF16, kind="ExternalOutput").ap()
        dbg_act = nc.dram_tensor("dbg_act", [128, IC * NT], BF16, kind="ExternalOutput").ap()
        dbg_es = nc.dram_tensor("dbg_es", [6, 128, 4 * E], F32, kind="ExternalOutput").ap()
        dbg_m = nc.dram_tensor("dbg_m", [3, 128, 1], F32, kind="ExternalOutput").ap()
    if DBG == 3:
        dbg_q = nc.dram_tensor("dbg_q", [D, NT], BF16, kind="ExternalOutput").ap()
        dbg_k = nc.dram_tensor("dbg_k", [D, NT], BF16, kind="ExternalOutput").ap()
        dbg_v = nc.dram_tensor("dbg_v", [NT, D], BF16, kind="ExternalOutput").ap()
        dbg_pt = nc.dram_tensor("dbg_pt", [128, 32 * SW], BF16, kind="ExternalOutput").ap()
        dbg_l = nc.dram_tensor("dbg_l", [1, SW], F32, kind="ExternalOutput").ap()

    ir = IR()
    RG = [[0, 1], [2, 3], [4, 5], [6, 7]]

    def OP(eng, fn, inc=None, n=1, waits=(), hard=()):
        return ir.op(eng, fn, inc=inc, n=n, waits=waits, hard=hard)


    def r3i(ap, a, b):
        return ap.rearrange("(a p) b -> p a b", p=128)

    def r3o(ap, a):
        return ap.rearrange("p (a b) -> p a b", a=a)

    with ExitStack() as top:
        sems = {s: top.enter_context(nc.semaphore(s))
                for s in ["ts", "vs", "ss", "gd", "yd", "cc"]}

        ones_bf = top.enter_context(nc.sbuf_tensor("ones_bf", [1, 128], BF16))
        ones_f = top.enter_context(nc.sbuf_tensor("ones_f", [1, 128], F32))
        onec_bf = top.enter_context(nc.sbuf_tensor("onec_bf", [128, 1], BF16))
        idn_bf = top.enter_context(nc.sbuf_tensor("idn_bf", [128, 128], BF16))
        gw_sb = top.enter_context(nc.sbuf_tensor("gw_sb", [128, DC * E], BF16))
        bq_sb = top.enter_context(nc.sbuf_tensor("bq_sb", [128, DC], F32))
        bk_sb = top.enter_context(nc.sbuf_tensor("bk_sb", [128, DC], F32))
        cv_f = top.enter_context(nc.sbuf_tensor("cv_f", [128, 16], F32))
        cv_b = top.enter_context(nc.sbuf_tensor("cv_b", [128, 16], BF16))
        for i, v in enumerate(CONST_VALS):
            nc.const_aps.register(F32, v, cv_f[:, i:i + 1])
            nc.const_aps.register(BF16, v, cv_b[:, i:i + 1])

        block = top.enter_context(nc.Block())

        cl = []
        for dst, src in [(ones_bf, ones_row), (ones_f, ones_row), (onec_bf, ones_col),
                         (idn_bf, idn_d), (bq_sb, bq_pp), (bk_sb, bk_pp),
                         (cv_f, cvals_d), (cv_b, cvals_d)]:
            cl.append(OP("gpsimd", lambda g, d=dst, s=src: g.dma_start(out=d[:, :], in_=s[:, :]),
                         inc="gd", n=16))
        cl.append(OP("gpsimd", lambda g: g.dma_start(
            out=r3o(gw_sb[:, :], DC), in_=r3i(gwT_d, DC, E)),
            inc="gd", n=16))
        consts = cl[-1]

        def ffn_scope_tensors(ctx, tag):
            t = {}
            t["x"] = ctx.enter_context(nc.sbuf_tensor(f"x_{tag}", [128, DC * NT], F32))
            t["xl"] = ctx.enter_context(nc.sbuf_tensor(f"xl_{tag}", [128, DC * NT], BF16))
            t["abuf"] = ctx.enter_context(nc.sbuf_tensor(f"abf_{tag}", [128, IC * NT], BF16))
            t["wg"] = ctx.enter_context(nc.sbuf_tensor(f"wg_{tag}", [128, 2 * DC * I], BF16))
            t["wu"] = ctx.enter_context(nc.sbuf_tensor(f"wu_{tag}", [128, 2 * DC * I], BF16))
            t["wd"] = ctx.enter_context(nc.sbuf_tensor(f"wd_{tag}", [128, 2 * IC * D], BF16))
            t["ws1"] = ctx.enter_context(nc.sbuf_tensor(f"ws1_{tag}", [128, 2 * DC * 128], BF16))
            t["ws2"] = ctx.enter_context(nc.sbuf_tensor(f"ws2_{tag}", [128, 2 * DC * 128], BF16))
            t["ws3"] = ctx.enter_context(nc.sbuf_tensor(f"ws3_{tag}", [128, 2 * DC * 128], BF16))
            t["crow"] = ctx.enter_context(nc.sbuf_tensor(f"crow_{tag}", [E, NT], BF16))
            t["oh"] = ctx.enter_context(nc.sbuf_tensor(f"oh_{tag}", [E, E * 128], BF16))
            t["cb"] = ctx.enter_context(nc.sbuf_tensor(f"cb_{tag}", [128, 2 * SW], F32))
            t["rows"] = t["cb"]
            t["aba"] = ctx.enter_context(nc.sbuf_tensor(f"aba_{tag}", [128, SW], F32))
            t["abb"] = ctx.enter_context(nc.sbuf_tensor(f"abb_{tag}", [128, SW], F32))
            t["silu"] = ctx.enter_context(nc.sbuf_tensor(f"sil_{tag}", [128, 2 * SW], F32))
            t["tts"] = ctx.enter_context(nc.sbuf_tensor(f"tts_{tag}", [128, 2 * SW], F32))
            t["xbf"] = ctx.enter_context(nc.sbuf_tensor(f"xbf_{tag}", [128, 2 * SW], BF16))
            t["x2bf"] = t["xbf"]
            t["rbf"] = ctx.enter_context(nc.sbuf_tensor(f"rbf_{tag}", [1, 2 * SW], BF16))
            t["esc"] = ctx.enter_context(nc.sbuf_tensor(f"esc_{tag}", [128, 12 * E], F32))
            for nm in ("nm1s", "nuac", "nui1", "m1s", "m2s"):
                t[nm] = [ctx.enter_context(nc.sbuf_tensor(f"{nm}{j}_{tag}", [128, 1], F32))
                         for j in range(4)]
            t["ctm"] = ctx.enter_context(nc.sbuf_tensor(f"ctm_{tag}", [128, 4 * E], BF16))
            t["qst"] = ctx.enter_context(nc.sbuf_tensor(f"qst_{tag}", [128, 2 * SW], BF16))
            t["pg"] = [ctx.enter_context(nc.psum_tensor(f"pg{i}_{tag}", [128, SW], F32)) for i in (0, 1)]
            t["pu"] = [ctx.enter_context(nc.psum_tensor(f"pu{i}_{tag}", [128, SW], F32)) for i in (0, 1)]
            t["pd"] = [ctx.enter_context(nc.psum_tensor(f"pd{i}_{tag}", [128, SW], F32)) for i in (0, 1)]
            t["pb"] = ctx.enter_context(nc.psum_tensor(f"pb_{tag}", [128, SW], F32))
            t["pbt"] = ctx.enter_context(nc.psum_tensor(f"pbt_{tag}", [E, 128], BF16))
            return t

        def emit_ln(t, xready_fn, trk):
            xl_done = {}
            bufp = {"c0": None, "c1": None, "q0": None, "q1": None}
            sbank = [t["pb"], t["pg"][0]]
            qbank = [t["pd"][0], t["pg"][1]]
            sb_user = {0: trk.get("pb_user"), 1: trk.get("pg0_user")}
            qb_user = {0: trk.get("pd0_user"), 1: trk.get("pg1_user")}
            pu_user = {0: trk.get("pu0_user"), 1: trk.get("pu1_user")}
            st = {}

            def emit_stats(s):
                d = {}
                sum_mm = None
                for kc in range(DC):
                    p2 = kc % 2
                    cp = OP("vector", lambda v, kc=kc, s=s, p2=p2: v.tensor_copy(
                        t["xbf"][:, p2 * SW:(p2 + 1) * SW],
                        t["x"][:, kc * NT + s * SW: kc * NT + (s + 1) * SW]),
                        inc="vs", waits=xready_fn(s) + [bufp[f"c{p2}"]])
                    sum_mm = OP("tensor", lambda tn, kc=kc, s=s, p2=p2: tn.matmul(
                        sbank[s % 2][0:1, :], onec_bf[:, :], t["xbf"][:, p2 * SW:(p2 + 1) * SW],
                        start=(kc == 0), stop=(kc == DC - 1)),
                        inc="ts", waits=[cp, consts] + ([sb_user[s % 2]] if kc == 0 else []))
                    bufp[f"c{p2}"] = sum_mm
                bufp["q0"], bufp["q1"] = bufp["c0"], bufp["c1"]
                sq_mm = None
                for kc in range(DC):
                    p2 = kc % 2
                    sq = OP("scalar", lambda sc, kc=kc, s=s, p2=p2: sc.square(
                        t["x2bf"][:, p2 * SW:(p2 + 1) * SW],
                        t["x"][:, kc * NT + s * SW: kc * NT + (s + 1) * SW]),
                        inc="ss", waits=xready_fn(s) + [bufp[f"q{p2}"]])
                    sq_mm = OP("tensor", lambda tn, kc=kc, s=s, p2=p2: tn.matmul(
                        qbank[s % 2][0:1, :], onec_bf[:, :], t["x2bf"][:, p2 * SW:(p2 + 1) * SW],
                        start=(kc == 0), stop=(kc == DC - 1)),
                        inc="ts", waits=[sq] + ([qb_user[s % 2]] if kc == 0 else []))
                    bufp[f"q{p2}"] = sq_mm
                bufp["c0"], bufp["c1"] = bufp["q0"], bufp["q1"]
                d["sum_mm"], d["sq_mm"] = sum_mm, sq_mm
                st[s] = d

            def emit_rows_bcast(s):
                d = st[s]
                pr = st.get(s - 1, {})
                R, RB = t["rows"], t["rbf"]
                nm = OP("vector", lambda v, s=s: v.tensor_scalar_mul(
                    RB[0:1, 0:SW], sbank[s % 2][0:1, :], -1.0 / D),
                    inc="vs", waits=[d["sum_mm"], pr.get("m2r"), pr.get("br"), pr.get("bb_mm")])
                m2r = OP("vector", lambda v: v.tensor_tensor(
                    R[0:1, 0:SW], RB[0:1, 0:SW], RB[0:1, 0:SW], ALU.mult),
                    inc="vs", waits=[nm, pr.get("ar")])
                vr = OP("vector", lambda v, s=s: v.scalar_tensor_tensor(
                    R[0:1, SW:2 * SW], qbank[s % 2][0:1, :], 1.0 / D, R[0:1, 0:SW],
                    ALU.mult, ALU.subtract),
                    inc="vs", waits=[d["sq_mm"], m2r, pr.get("lnv")])
                lnv = OP("scalar", lambda sc: sc.activation(
                    R[0:1, 0:SW], R[0:1, SW:2 * SW], AF.Ln, bias=1e-5),
                    inc="ss", waits=[vr, m2r])
                ar = OP("scalar", lambda sc: sc.activation(
                    RB[0:1, SW:2 * SW], R[0:1, 0:SW], AF.Exp, scale=-0.5),
                    inc="ss", waits=[lnv, pr.get("ba_mm")])
                br = OP("vector", lambda v: v.tensor_tensor(
                    RB[0:1, 0:SW], RB[0:1, SW:2 * SW], RB[0:1, 0:SW], ALU.mult),
                    inc="vs", waits=[ar, nm, pr.get("bb_mm")])
                ba_mm = OP("tensor", lambda tn: tn.matmul(
                    t["pu"][0][:, :], ones_bf[:, :], RB[0:1, SW:2 * SW], start=True, stop=True),
                    inc="ts", waits=[ar, nm, pu_user[0]])
                bb_mm = OP("tensor", lambda tn: tn.matmul(
                    t["pu"][1][:, :], ones_bf[:, :], RB[0:1, 0:SW], start=True, stop=True),
                    inc="ts", waits=[br, vr, pu_user[1]])
                ca = OP("scalar", lambda sc: sc.copy(t["aba"][:, :], t["pu"][0][:, :]),
                        inc="ss", waits=[ba_mm, pr.get("t1_last")])
                cb_ = OP("scalar", lambda sc: sc.copy(t["abb"][:, :], t["pu"][1][:, :]),
                         inc="ss", waits=[bb_mm, pr.get("t2_last")])
                pu_user[0], pu_user[1] = ca, cb_
                d.update(nm=nm, m2r=m2r, vr=vr, lnv=lnv, ar=ar, br=br,
                         ba_mm=ba_mm, bb_mm=bb_mm, ca=ca, cb_=cb_)
                sb_user[s % 2] = nm
                qb_user[s % 2] = vr

            def emit_norm(s):
                d = st[s]
                t1 = t2 = None
                for kc in range(DC):
                    p2 = kc % 2
                    t1 = OP("vector", lambda v, kc=kc, s=s, p2=p2: v.tensor_tensor(
                        t["tts"][:, p2 * SW:(p2 + 1) * SW],
                        t["x"][:, kc * NT + s * SW: kc * NT + (s + 1) * SW],
                        t["aba"][:, :], ALU.mult), inc="vs", waits=[d["ca"]])
                    t2 = OP("vector", lambda v, kc=kc, s=s, p2=p2: v.tensor_tensor(
                        t["xl"][:, kc * NT + s * SW: kc * NT + (s + 1) * SW],
                        t["tts"][:, p2 * SW:(p2 + 1) * SW], t["abb"][:, :], ALU.add),
                        inc="vs", waits=[d["cb_"], t1])
                    xl_done[(kc, s)] = t2
                d["t1_last"], d["t2_last"] = t1, t2

            emit_stats(0)
            emit_stats(1)
            emit_rows_bcast(0)
            emit_stats(2)
            emit_norm(0)
            emit_rows_bcast(1)
            emit_stats(3)
            emit_norm(1)
            emit_rows_bcast(2)
            emit_norm(2)
            emit_rows_bcast(3)
            emit_norm(3)
            xl_done["all"] = st[3]["t2_last"]
            trk["pb_user"] = st[2]["nm"]
            trk["pg0_user"] = st[3]["nm"]
            trk["pd0_user"] = st[2]["vr"]
            trk["pg1_user"] = st[3]["vr"]
            trk["pu0_user"] = st[3]["ca"]
            trk["pu1_user"] = st[3]["cb_"]
            return xl_done

        def emit_logits(t, xl_done, trk):
            crow_done = {}
            lbank = [t["pb"], t["pd"][0], t["pg"][0], t["pg"][1]]
            lb_user = {0: trk.get("pb_user"), 1: trk.get("pd0_user"),
                       2: trk.get("pg0_user"), 3: trk.get("pg1_user")}
            prev_cpy = None
            prev_tp = None
            for g in range(4):
                tiles = [g * 4 + j for j in range(4)]
                lmm = {}
                for j, tt in enumerate(tiles):
                    s = tt // 4
                    mm = None
                    for kc in range(DC):
                        mm = OP("tensor", lambda tn, kc=kc, tt=tt, j=j: tn.matmul(
                            lbank[j][:, 0:E],
                            t["xl"][:, kc * NT + tt * 128: kc * NT + (tt + 1) * 128],
                            gw_sb[:, kc * E:(kc + 1) * E],
                            start=(kc == 0), stop=(kc == DC - 1)),
                            inc="ts",
                            waits=([xl_done[(DC - 1, s)], consts, lb_user[j]]
                                   if kc == 0 else []))
                    lmm[tt] = mm
                ES = t["esc"]
                SL = lambda j, i: ES[:, (j * 3 + i) * E:(j * 3 + i + 1) * E]
                h = [dict() for _ in range(4)]
                for j, tt in enumerate(tiles):
                    h[j]["nm1"] = OP("vector", lambda v, j=j: v.reduce_max(
                        t["nm1s"][j][:, :], lbank[j][:, 0:E], AX, negate=True),
                        inc="vs", waits=[lmm[tt]])
                for j, tt in enumerate(tiles):
                    h[j]["ex"] = OP("scalar", lambda sc, j=j: sc.activation(
                        SL(j, 0), lbank[j][:, 0:E], AF.Exp,
                        bias=t["nm1s"][j][:, :], accum_out=t["nuac"][j][:, :]),
                        inc="ss", waits=[h[j]["nm1"]])
                    lb_user[j] = h[j]["ex"]
                for j in range(4):
                    h[j]["nui"] = OP("vector", lambda v, j=j: v.reciprocal(
                        t["nui1"][j][:, :], t["nuac"][j][:, :]), inc="vs", waits=[h[j]["ex"]])
                for j in range(4):
                    h[j]["m1e"] = OP("vector", lambda v, j=j: v.reduce_max(
                        t["m1s"][j][:, :], SL(j, 0), AX), inc="vs", waits=[h[j]["ex"]])
                for j in range(4):
                    h[j]["d1"] = OP("vector", lambda v, j=j: v.tensor_scalar(
                        SL(j, 1), SL(j, 0), t["m1s"][j][:, :], 0.0, ALU.subtract, ALU.add),
                        inc="vs", hard=[h[j]["m1e"]])
                for j in range(4):
                    h[j]["tnm"] = OP("vector", lambda v, j=j: v.tensor_scalar(
                        SL(j, 2), SL(j, 1), -1e30, 1.0, ALU.mult, ALU.min),
                        inc="vs", waits=[h[j]["d1"]])
                for j in range(4):
                    h[j]["e2"] = OP("vector", lambda v, j=j: v.tensor_tensor(
                        SL(j, 1), SL(j, 0), SL(j, 2), ALU.mult), inc="vs", waits=[h[j]["tnm"]])
                for j in range(4):
                    h[j]["m2e"] = OP("vector", lambda v, j=j: v.reduce_max(
                        t["m2s"][j][:, :], SL(j, 1), AX), inc="vs", waits=[h[j]["e2"]])
                for j in range(4):
                    h[j]["d2"] = OP("vector", lambda v, j=j: v.tensor_scalar(
                        SL(j, 2), SL(j, 0), t["m2s"][j][:, :], 0.0, ALU.subtract, ALU.add),
                        inc="vs", hard=[h[j]["m2e"]])
                for j in range(4):
                    h[j]["u0"] = OP("vector", lambda v, j=j: v.tensor_scalar(
                        SL(j, 1), SL(j, 2), 1e30, 1.0, ALU.mult, ALU.add),
                        inc="vs", waits=[h[j]["d2"]])
                for j in range(4):
                    h[j]["msk"] = OP("vector", lambda v, j=j: v.tensor_scalar(
                        SL(j, 2), SL(j, 1), 0.0, 1.0, ALU.max, ALU.min),
                        inc="vs", waits=[h[j]["u0"]])
                for j in range(4):
                    h[j]["w_"] = OP("vector", lambda v, j=j: v.tensor_tensor(
                        SL(j, 1), SL(j, 0), SL(j, 2), ALU.mult), inc="vs", waits=[h[j]["msk"]])
                for j, tt in enumerate(tiles):
                    ctm = OP("vector", lambda v, j=j: v.tensor_scalar(
                        t["ctm"][:, j * E:(j + 1) * E], SL(j, 1), t["nui1"][j][:, :], 0.5,
                        ALU.mult, ALU.mult),
                        inc="vs", waits=[h[j]["w_"], prev_tp], hard=[h[j]["nui"]])
                    tp = OP("tensor", lambda tn, j=j: tn.transpose(
                        t["pbt"][0:E, :], t["ctm"][:, j * E:(j + 1) * E], idn_bf[:, :]),
                        inc="ts", waits=[ctm, prev_cpy, consts])
                    prev_tp = tp
                    cpy = OP("vector", lambda v, tt=tt: v.tensor_copy(
                        t["crow"][:, tt * 128:(tt + 1) * 128], t["pbt"][0:E, :]),
                        inc="vs", waits=[tp])
                    prev_cpy = cpy
                    crow_done[tt] = cpy
            for s in range(NS):
                crow_done[("s", s)] = crow_done[s * 4 + 3]
            trk["pb_user"] = lb_user[0]
            trk["pd0_user"] = lb_user[1]
            trk["pg0_user"] = lb_user[2]
            trk["pg1_user"] = lb_user[3]
            return crow_done

        def emit_ffn(t, tag, xready_fn, trk):
            xl_done = emit_ln(t, xready_fn, trk)
            crow_done = emit_logits(t, xl_done, trk)

            oh_ld = OP("gpsimd", lambda g: g.dma_start(out=t["oh"][:, :], in_=oh_d[:, :]),
                       inc="gd", n=16, waits=[trk.get("wgu_user0")])
            wg_ld, wu_ld, wd_ld = {}, {}, {}
            gu_user = {0: None, 1: None}
            wd_user = None
            wd_user_par = {0: None, 1: None}
            xadd = {}
            cb_ev = {}
            pg_free = {0: trk.get("pg0_user"), 1: trk.get("pg1_user")}
            pu_free = {0: trk.get("pu0_user"), 1: trk.get("pu1_user")}
            pd_free = {0: trk.get("pd0_user"), 1: trk.get("pd1_user")}
            act_done = {}
            prev_cb = trk.get("pb_user")

            def load_expert(e, dep_g):
                wg_ld[e] = OP("gpsimd", lambda g, e=e: g.dma_start(
                    out=r3o(t["wg"][:, (e % 2) * DC * I:(e % 2 + 1) * DC * I], DC),
                    in_=r3i(wg_all[e], DC, I)),
                    inc="gd", n=16, waits=[dep_g])
                wu_ld[e] = OP("gpsimd", lambda g, e=e: g.dma_start(
                    out=r3o(t["wu"][:, (e % 2) * DC * I:(e % 2 + 1) * DC * I], DC),
                    in_=r3i(wu_all[e], DC, I)),
                    inc="gd", n=16, waits=[dep_g])

            def load_wd(e, dep_d):
                wd_ld[e] = OP("gpsimd", lambda g, e=e: g.dma_start(
                    out=r3o(t["wd"][:, (e % 2) * IC * D:(e % 2 + 1) * IC * D], IC),
                    in_=r3i(wd_all[e], IC, D)),
                    inc="gd", n=16, waits=[dep_d])

            load_expert(0, trk.get("wgu_user0"))
            load_expert(1, trk.get("wgu_user1"))
            load_wd(0, trk.get("wd_user0"))
            load_wd(1, trk.get("wd_user1"))

            cb_user = {0: None, 1: None}
            silu_reader = {0: trk.get("silu_reader0"), 1: trk.get("silu_reader1")}
            dbg_h = {}
            for e in range(E):
                eb = (e % 2) * DC * I
                for s in range(NS):
                    s2 = s % 2
                    mm = OP("tensor", lambda tn, e=e, s=s: tn.matmul(
                        t["pb"][:, :], t["oh"][:, e * 128:(e + 1) * 128],
                        t["crow"][:, s * SW:(s + 1) * SW], start=True, stop=True),
                        inc="ts",
                        waits=[crow_done[("s", s)], prev_cb, cb_user[s2], oh_ld]
                        + ([dbg_h.get("cb"), dbg_h.get("act")] if DBG and tag == "A" else []))
                    ev = OP("scalar", lambda sc, s2=s2: sc.copy(
                        t["cb"][:, s2 * SW:(s2 + 1) * SW], t["pb"][:, :]),
                        inc="ss", waits=[mm])
                    cb_ev[(e, s)] = ev
                    prev_cb = ev
                    for ic in range(IC):
                        p2 = (s * IC + ic) % 2
                        gmm = None
                        for kc in range(DC):
                            gmm = OP("tensor", lambda tn, kc=kc, ic=ic, s=s, p2=p2, eb=eb: tn.matmul(
                                t["pg"][p2][:, :],
                                t["wg"][:, eb + kc * I + ic * 128: eb + kc * I + (ic + 1) * 128],
                                t["xl"][:, kc * NT + s * SW: kc * NT + (s + 1) * SW],
                                start=(kc == 0), stop=(kc == DC - 1)),
                                inc="ts",
                                waits=([wg_ld[e], pg_free[p2], xl_done[(DC - 1, s)]]
                                       if kc == 0 else []))
                        sil = OP("scalar", lambda sc, p2=p2: sc.activation(
                            t["silu"][:, p2 * SW:(p2 + 1) * SW], t["pg"][p2][:, :], AF.Silu),
                            inc="ss", waits=[gmm, silu_reader.get(p2)])
                        pg_free[p2] = sil
                        umm = None
                        for kc in range(DC):
                            umm = OP("tensor", lambda tn, kc=kc, ic=ic, s=s, p2=p2, eb=eb: tn.matmul(
                                t["pu"][p2][:, :],
                                t["wu"][:, eb + kc * I + ic * 128: eb + kc * I + (ic + 1) * 128],
                                t["xl"][:, kc * NT + s * SW: kc * NT + (s + 1) * SW],
                                start=(kc == 0), stop=(kc == DC - 1)),
                                inc="ts",
                                waits=([wu_ld[e], pu_free[p2]] if kc == 0 else []))
                        tmul = OP("vector", lambda v, p2=p2, s2=s2: v.tensor_tensor(
                            t["tts"][:, p2 * SW:(p2 + 1) * SW], t["pu"][p2][:, :],
                            t["cb"][:, s2 * SW:(s2 + 1) * SW], ALU.mult),
                            inc="vs", waits=[umm, cb_ev[(e, s)]])
                        amul = OP("vector", lambda v, p2=p2, ic=ic, s=s: v.tensor_tensor(
                            t["abuf"][:, ic * NT + s * SW: ic * NT + (s + 1) * SW],
                            t["tts"][:, p2 * SW:(p2 + 1) * SW],
                            t["silu"][:, p2 * SW:(p2 + 1) * SW], ALU.mult),
                            inc="vs", waits=[tmul, sil])
                        act_done[(e, ic, s)] = amul
                        pu_free[p2] = tmul
                        cb_user[s2] = tmul
                        silu_reader[p2] = amul
                if DBG and tag == "A" and e == 0:
                    dbg_h["cb"] = OP("sync", lambda sy: sy.dma_start(
                        out=dbg_cb[:, :], in_=t["cb"][:, :]),
                        inc="yd", n=16, waits=[cb_ev[(0, 0)], cb_ev[(0, 1)]])
                    dbg_h["act"] = OP("sync", lambda sy: sy.dma_start(
                        out=dbg_act[:, :], in_=t["abuf"][:, 0:IC * NT]),
                        inc="yd", n=16,
                        waits=[act_done[(0, i2, s2_)] for i2 in range(IC) for s2_ in range(NS)])
                    dbg_h["crow"] = OP("sync", lambda sy: sy.dma_start(
                        out=dbg_crow[:, :], in_=t["crow"][:, :]),
                        inc="yd", n=16, waits=[crow_done[("s", 3)]])
                    dbg_h["xl"] = OP("sync", lambda sy: sy.dma_start(
                        out=dbg_xl[:, :], in_=t["xl"][:, 0:NT]),
                        inc="yd", n=16, waits=[xl_done["all"]])
                gu_user[e % 2] = umm
                if e + 2 < E:
                    load_expert(e + 2, gu_user[e % 2])
                dmm = None
                eb_d = (e % 2) * IC * D
                for hc in range(HC):
                    for s in range(NS):
                        p2 = (hc * NS + s) % 2
                        for ic in range(IC):
                            dmm = OP("tensor", lambda tn, hc=hc, s=s, ic=ic, p2=p2, eb_d=eb_d: tn.matmul(
                                t["pd"][p2][:, :],
                                t["wd"][:, eb_d + ic * D + hc * 128: eb_d + ic * D + (hc + 1) * 128],
                                t["abuf"][:, ic * NT + s * SW: ic * NT + (s + 1) * SW],
                                start=(ic == 0), stop=(ic == IC - 1)),
                                inc="ts",
                                waits=([wd_ld[e], pd_free[p2], act_done[(e, IC - 1, s)]]
                                       if ic == 0 else []))
                        xa = OP("vector", lambda v, hc=hc, s=s, p2=p2: v.tensor_tensor(
                            t["x"][:, hc * NT + s * SW: hc * NT + (s + 1) * SW],
                            t["pd"][p2][:, :],
                            t["x"][:, hc * NT + s * SW: hc * NT + (s + 1) * SW], ALU.add),
                            inc="vs", waits=[dmm, xadd.get((e - 1, hc, s))])
                        xadd[(e, hc, s)] = xa
                        pd_free[p2] = xa
                wd_user = dmm
                wd_user_par[e % 2] = dmm
                if e + 2 < E:
                    load_wd(e + 2, dmm)

            ws1_ld, ws2_ld, ws3_ld = {}, {}, {}
            ws12_user = {0: trk.get("ws12_user0"), 1: trk.get("ws12_user1")}
            out_h = {}
            prev_half_down = None

            def load_ws12(th, oc):
                dep = ws12_user.get(oc % 2)
                ws1_ld[(th, oc)] = OP("gpsimd", lambda g, oc=oc: g.dma_start(
                    out=r3o(t["ws1"][:, (oc % 2) * DC * 128:(oc % 2 + 1) * DC * 128], DC),
                    in_=r3i(wsg_d[:, oc * 128:(oc + 1) * 128], DC, 128)),
                    inc="gd", n=16, waits=[dep])
                ws2_ld[(th, oc)] = OP("gpsimd", lambda g, oc=oc: g.dma_start(
                    out=r3o(t["ws2"][:, (oc % 2) * DC * 128:(oc % 2 + 1) * DC * 128], DC),
                    in_=r3i(wsu_d[:, oc * 128:(oc + 1) * 128], DC, 128)),
                    inc="gd", n=16, waits=[dep])

            ws3_user = None

            def load_ws3(th, hc):
                ws3_ld[(th, hc)] = OP("gpsimd", lambda g, hc=hc: g.dma_start(
                    out=r3o(t["ws3"][:, (hc % 2) * DC * 128:(hc % 2 + 1) * DC * 128], DC),
                    in_=r3i(wsd_d[:, hc * 128:(hc + 1) * 128], DC, 128)),
                    inc="gd", n=16, waits=[ws3_user])

            for th in range(2):
                sg_done = {}
                load_ws12(th, 0)
                load_ws12(th, 1)
                for oc in range(HC):
                    ob = (oc % 2) * DC * 128
                    for s in (2 * th, 2 * th + 1):
                        p2 = (oc * 2 + s) % 2
                        gmm = None
                        for kc in range(DC):
                            gmm = OP("tensor", lambda tn, oc=oc, s=s, kc=kc, p2=p2, ob=ob: tn.matmul(
                                t["pg"][p2][:, :],
                                t["ws1"][:, ob + kc * 128: ob + (kc + 1) * 128],
                                t["xl"][:, kc * NT + s * SW: kc * NT + (s + 1) * SW],
                                start=(kc == 0), stop=(kc == DC - 1)),
                                inc="ts",
                                waits=([ws1_ld[(th, oc)], pg_free[p2]] if kc == 0 else []))
                        sil = OP("scalar", lambda sc, p2=p2: sc.activation(
                            t["silu"][:, p2 * SW:(p2 + 1) * SW], t["pg"][p2][:, :], AF.Silu),
                            inc="ss", waits=[gmm, silu_reader.get(p2)])
                        pg_free[p2] = sil
                        umm = None
                        for kc in range(DC):
                            umm = OP("tensor", lambda tn, oc=oc, s=s, kc=kc, p2=p2, ob=ob: tn.matmul(
                                t["pu"][p2][:, :],
                                t["ws2"][:, ob + kc * 128: ob + (kc + 1) * 128],
                                t["xl"][:, kc * NT + s * SW: kc * NT + (s + 1) * SW],
                                start=(kc == 0), stop=(kc == DC - 1)),
                                inc="ts",
                                waits=([ws2_ld[(th, oc)], pu_free[p2]] if kc == 0 else []))
                        sg = OP("vector", lambda v, oc=oc, s=s, p2=p2: v.scalar_tensor_tensor(
                            t["abuf"][:, oc * (NT // 2) + (s % 2) * SW: oc * (NT // 2) + (s % 2 + 1) * SW],
                            t["pu"][p2][:, :], 0.5, t["silu"][:, p2 * SW:(p2 + 1) * SW],
                            ALU.mult, ALU.mult),
                            inc="vs", waits=[umm, sil, prev_half_down])
                        sg_done[(oc, s)] = sg
                        pu_free[p2] = sg
                        silu_reader[p2] = sg
                    ws12_user[oc % 2] = umm
                    if oc + 2 < HC:
                        load_ws12(th, oc + 2)

                load_ws3(th, 0)
                load_ws3(th, 1)
                for hc in range(HC):
                    hb = (hc % 2) * DC * 128
                    for s in (2 * th, 2 * th + 1):
                        p2 = (hc * 2 + s) % 2
                        dmm = None
                        for oc in range(DC):
                            dmm = OP("tensor", lambda tn, hc=hc, s=s, oc=oc, p2=p2, hb=hb: tn.matmul(
                                t["pd"][p2][:, :],
                                t["ws3"][:, hb + oc * 128: hb + (oc + 1) * 128],
                                t["abuf"][:, oc * (NT // 2) + (s % 2) * SW: oc * (NT // 2) + (s % 2 + 1) * SW],
                                start=(oc == 0), stop=(oc == DC - 1)),
                                inc="ts",
                                waits=([ws3_ld[(th, hc)], pd_free[p2], sg_done[(DC - 1, s)]]
                                       if oc == 0 else []))
                        xa = OP("vector", lambda v, hc=hc, s=s, p2=p2: v.tensor_tensor(
                            t["x"][:, hc * NT + s * SW: hc * NT + (s + 1) * SW],
                            t["pd"][p2][:, :],
                            t["x"][:, hc * NT + s * SW: hc * NT + (s + 1) * SW], ALU.add),
                            inc="vs", waits=[dmm, xadd[(E - 1, hc, s)]])
                        out_h[(hc, s)] = xa
                        pd_free[p2] = xa
                    ws3_user = dmm
                    if hc + 2 < HC:
                        load_ws3(th, hc + 2)
                prev_half_down = dmm

            trk.update({"silu_reader0": silu_reader[0], "silu_reader1": silu_reader[1]})
            trk.update({"pg0_user": pg_free[0], "pg1_user": pg_free[1],
                        "pu0_user": pu_free[0], "pu1_user": pu_free[1],
                        "pd0_user": pd_free[0], "pd1_user": pd_free[1],
                        "wgu_user0": gu_user[0], "wgu_user1": gu_user[1],
                        "wd_user0": wd_user_par[0], "wd_user1": wd_user_par[1],
                        "ws12_user0": ws12_user[0], "ws12_user1": ws12_user[1],
                        "ws3_user": ws3_user})
            out_h["all"] = xa
            return out_h

        phase = {}

        with ExitStack() as sA:
            t = ffn_scope_tensors(sA, "A")
            t["is_A"] = True
            bvr_bf = sA.enter_context(nc.sbuf_tensor("bvr_bf", [1, D], BF16))
            bvr_ld = OP("gpsimd", lambda g: g.dma_start(out=bvr_bf[:, :], in_=bv_row[:, :]),
                        inc="gd", n=16)
            trk = {}
            x_lds = {}
            for s4 in range(NS):
                x_lds[s4] = OP("sync", lambda sy, s4=s4: sy.dma_start(
                    out=t["x"][:, :].rearrange("p (kc n) -> p kc n", kc=DC)[:, :, s4 * SW:(s4 + 1) * SW],
                    in_=r3i(x_in, DC, NT)[:, :, s4 * SW:(s4 + 1) * SW]),
                    inc="yd", n=16)
            x_ld = x_lds[NS - 1]
            if "ffn1" in PHASES:
                f1 = emit_ffn(t, "A", lambda s: [x_lds[s]], trk)
            else:
                f1 = {"all": x_ld}
                for hc in range(HC):
                    for s in range(NS):
                        f1[(hc, s)] = x_ld

            if "attn" in PHASES:
                f1all = [f1[(h2, s2)] for h2 in range(HC) for s2 in range(NS)]
                xl2 = emit_ln(t, lambda s: [f1[(h2, s)] for h2 in range(HC)], trk)
                bvmm, bvc = [], []
                for dg in (0, 1):
                    mm = OP("tensor", lambda tn, dg=dg: tn.matmul(
                        t["pb"][:, :], ones_bf[:, :], bvr_bf[0:1, dg * SW:(dg + 1) * SW],
                        start=True, stop=True),
                        inc="ts", waits=[consts, bvr_ld, xl2["all"], trk.get("pb_user")])
                    ev = OP("scalar", lambda sc, dg=dg: sc.copy(
                        t["cb"][:, dg * SW:(dg + 1) * SW], t["pb"][:, :]),
                        inc="ss", waits=[mm])
                    bvmm.append(mm)
                    bvc.append(ev)
                    trk["pb_user"] = ev

                wq_ld, wk_ld = {}, {}
                qk_user = {0: trk.get("ws12_user0"), 1: trk.get("ws12_user1")}

                def load_qk(oc):
                    dep = qk_user.get(oc % 2)
                    wq_ld[oc] = OP("gpsimd", lambda g, oc=oc: g.dma_start(
                        out=r3o(t["ws1"][:, (oc % 2) * DC * 128:(oc % 2 + 1) * DC * 128], DC),
                        in_=r3i(wqT_d[:, oc * 128:(oc + 1) * 128], DC, 128)),
                        inc="gd", n=16, waits=[dep])
                    wk_ld[oc] = OP("gpsimd", lambda g, oc=oc: g.dma_start(
                        out=r3o(t["ws2"][:, (oc % 2) * DC * 128:(oc % 2 + 1) * DC * 128], DC),
                        in_=r3i(wkT_d[:, oc * 128:(oc + 1) * 128], DC, 128)),
                        inc="gd", n=16, waits=[dep])

                load_qk(0)
                load_qk(1)
                pq_free = {0: trk.get("pg0_user"), 1: trk.get("pg1_user")}
                qst_free = {0: None, 1: None}
                k_stores, v_stores = [], []
                for oc in range(DC):
                    ob = (oc % 2) * DC * 128
                    last_mm = None
                    for which in ("q", "k"):
                        wld = wq_ld if which == "q" else wk_ld
                        wbuf = t["ws1"] if which == "q" else t["ws2"]
                        bias_sb = bq_sb if which == "q" else bk_sb
                        for s in range(NS):
                            p2 = (oc * NS + s) % 2
                            mm = None
                            for kc in range(DC):
                                mm = OP("tensor", lambda tn, wbuf=wbuf, ob=ob, kc=kc, s=s, p2=p2: tn.matmul(
                                    t["pg"][p2][:, :],
                                    wbuf[:, ob + kc * 128: ob + (kc + 1) * 128],
                                    t["xl"][:, kc * NT + s * SW: kc * NT + (s + 1) * SW],
                                    start=(kc == 0), stop=(kc == DC - 1)),
                                    inc="ts",
                                    waits=([wld[oc], pq_free[p2], xl2[(DC - 1, s)]]
                                           if kc == 0 else []))
                            ev = OP("scalar", lambda sc, p2=p2, oc=oc, bias_sb=bias_sb: sc.activation(
                                t["qst"][:, p2 * SW:(p2 + 1) * SW], t["pg"][p2][:, :],
                                AF.Identity, bias=bias_sb[:, oc:oc + 1]),
                                inc="ss", waits=[mm, qst_free[p2], consts])
                            pq_free[p2] = ev
                            if which == "q":
                                st = OP("sync", lambda sy, oc=oc, s=s, p2=p2: sy.dma_start(
                                    out=q_hbm[oc * 128:(oc + 1) * 128, s * SW:(s + 1) * SW],
                                    in_=t["qst"][:, p2 * SW:(p2 + 1) * SW]),
                                    inc="yd", n=16, waits=[ev])
                            else:
                                st = OP("sync", lambda sy, oc=oc, s=s, p2=p2: sy.dma_start(
                                    out=kv3[oc * 128:(oc + 1) * 128, s // 2,
                                            (s % 2) * SW:(s % 2) * SW + SW],
                                    in_=t["qst"][:, p2 * SW:(p2 + 1) * SW]),
                                    inc="yd", n=16, waits=[ev])
                            qst_free[p2] = st
                            if which == "k":
                                k_stores.append(st)
                            last_mm = mm
                    qk_user[oc % 2] = last_mm
                    if oc + 2 < DC:
                        load_qk(oc + 2)

                wv_ld = {}
                wv_user = {0: trk.get("wgu_user0"), 1: trk.get("wgu_user1")}

                def load_wv(dg):
                    wv_ld[dg] = OP("gpsimd", lambda g, dg=dg: g.dma_start(
                        out=r3o(t["wg"][:, (dg % 2) * DC * I:(dg % 2 + 1) * DC * I], DC),
                        in_=r3i(wvT_d[:, dg * SW:(dg + 1) * SW], DC, SW)),
                        inc="gd", n=16, waits=[wv_user.get(dg % 2)])

                load_wv(0)
                load_wv(1)
                pv_free = {0: trk.get("pu0_user"), 1: trk.get("pu1_user")}
                for tt in range(NTT):
                    for dg in range(2):
                        p2 = (tt * 2 + dg) % 2
                        db = (dg % 2) * DC * I
                        mm = None
                        for kc in range(DC):
                            mm = OP("tensor", lambda tn, tt=tt, dg=dg, kc=kc, p2=p2, db=db: tn.matmul(
                                t["pu"][p2][:, :],
                                t["xl"][:, kc * NT + tt * 128: kc * NT + (tt + 1) * 128],
                                t["wg"][:, db + kc * I: db + kc * I + SW],
                                start=(kc == 0), stop=(kc == DC - 1)),
                                inc="ts",
                                waits=([wv_ld[dg], pv_free[p2], xl2["all"]] if kc == 0 else []))
                        ev = OP("vector", lambda v, p2=p2, dg=dg: v.tensor_tensor(
                            t["qst"][:, p2 * SW:(p2 + 1) * SW], t["pu"][p2][:, :],
                            t["cb"][:, dg * SW:(dg + 1) * SW], ALU.add),
                            inc="vs", waits=[mm, bvc[dg], qst_free[p2]])
                        pv_free[p2] = ev
                        st = OP("sync", lambda sy, tt=tt, dg=dg, p2=p2: sy.dma_start(
                            out=kv_in[NT + tt * 128: NT + (tt + 1) * 128, dg * SW:(dg + 1) * SW],
                            in_=t["qst"][:, p2 * SW:(p2 + 1) * SW]),
                            inc="yd", n=16, waits=[ev])
                        qst_free[p2] = st
                        v_stores.append(st)

                phase["a_end"] = mm
                park = OP("sync", lambda sy: sy.dma_start(
                    out=r3i(xp_d, DC, NT), in_=r3o(t["x"][:, :], DC)),
                    inc="yd", n=16, waits=[xl2["all"]] + f1all)
                if DBG == 3:
                    OP("sync", lambda sy: sy.dma_start(out=dbg_q[:, :], in_=q_hbm[:, :]),
                       inc="yd", n=16, waits=[k_stores[-1], v_stores[-1]])
                    OP("sync", lambda sy: sy.dma_start(out=dbg_k[:, :], in_=k_in[:, :]),
                       inc="yd", n=16)
                    OP("sync", lambda sy: sy.dma_start(out=dbg_v[:, :], in_=v_in[:, :]),
                       inc="yd", n=16)
                cc1 = OP("gpsimd", lambda g: g.collective_compute(
                    "AllGather", ALU.bypass, replica_groups=RG,
                    ins=[kv_in.opt()], outs=[kv_out.opt()]),
                    inc="cc", waits=[v_stores[-1], phase["a_end"], park])
                phase["park"], phase["cck"], phase["ccv"] = park, cc1, cc1
            else:
                ystore = OP("sync", lambda sy: sy.dma_start(
                    out=r3i(y_out, DC, NT), in_=r3o(t["x"][:, :], DC)),
                    inc="yd", n=16,
                    waits=[f1[(h2, s2)] for h2 in range(HC) for s2 in range(NS)])
                OP("sync", None, waits=[ystore])

        if "attn" in PHASES:
            with ExitStack() as sB:
                kf = sB.enter_context(nc.sbuf_tensor("kf", [128, DC * 2 * NT], BF16))
                vt = sB.enter_context(nc.sbuf_tensor("vt", [128, 32 * 1024], BF16))
                pt = sB.enter_context(nc.sbuf_tensor("pt", [128, 48 * SW], BF16))
                qsb = sB.enter_context(nc.sbuf_tensor("qsb", [128, 2 * DC * SW], BF16))
                xsts = sB.enter_context(nc.sbuf_tensor("xsts", [128, 2 * SW], F32))
                xnw = sB.enter_context(nc.sbuf_tensor("xnw", [128, 3 * SW], F32))
                lrow = sB.enter_context(nc.sbuf_tensor("lrow", [1, SW], F32))
                lbc = sB.enter_context(nc.sbuf_tensor("lbc", [128, SW], F32))
                ps = [sB.enter_context(nc.psum_tensor(f"ps{i}", [128, SW], F32)) for i in (0, 1)]
                pav = [sB.enter_context(nc.psum_tensor(f"pv{i}", [128, SW], F32)) for i in (0, 1)]
                pl = sB.enter_context(nc.psum_tensor("pl", [128, SW], F32))

                cck, ccv, park = phase["cck"], phase["ccv"], phase["park"]
                regs = {}

                def setup_regs(sy):
                    pr = sy.register("prank").__enter__()
                    sy.cc_rank_ld(pr, replica_groups=RG)
                    iv = sy.register("pinv").__enter__()
                    sy.reg_alu(iv, pr, 1, ALU.bitwise_xor)
                    regs["inv"] = sy.snap(iv, min_val=0, max_val=1)
                    vb = sy.register("vbase").__enter__()
                    sy.reg_mul(vb, iv, 2 * NT)
                    sy.reg_add(vb, vb, NT)
                    regs["vb"] = sy.snap(vb, min_val=NT, max_val=3 * NT)

                OP("sync", setup_regs)
                q_lds = {}

                def load_q(qg, deps):
                    q_lds[qg] = OP("sync", lambda sy, qg=qg: sy.dma_start(
                        out=r3o(qsb[:, (qg % 2) * DC * SW:(qg % 2 + 1) * DC * SW], DC),
                        in_=r3i(q_hbm[:, qg * SW:(qg + 1) * SW], DC, SW)),
                        inc="yd", n=16, waits=deps)

                pre_dep = [phase["a_end"], park]
                load_q(0, pre_dep)
                load_q(1, pre_dep)
                own_last = None
                for kc in range(DC):
                    own_last = OP("sync", lambda sy, kc=kc: sy.dma_start(
                        out=kf[:, kc * 2 * NT: kc * 2 * NT + NT]
                            .rearrange("p (h c) -> p h c", h=2),
                        in_=kv3[kc * 128:(kc + 1) * 128, :, :]),
                        inc="yd", n=16, waits=pre_dep)
                own_last = OP("sync", lambda sy: sy.dma_start(
                    out=r3o(vt[:, 0:16 * 1024], 16),
                    in_=r3i(kv_in[NT:2 * NT, :], 16, 1024)),
                    inc="yd", n=16, waits=pre_dep)
                phase["cck"]["waits"].append(own_last)

                kfp_last = None
                for kc in range(DC):
                    kfp_last = OP("sync", lambda sy, kc=kc: sy.dma_start(
                        out=kf[:, kc * 2 * NT + NT: kc * 2 * NT + 2 * NT]
                            .rearrange("p (h c) -> p h c", h=2),
                        in_=kvo4[bass.ds(regs["inv"], 1), kc * 128:(kc + 1) * 128, :, :]),
                        inc="yd", n=16, waits=[cck])
                vtp = OP("sync", lambda sy: sy.dma_start(
                    out=r3o(vt[:, 16 * 1024:32 * 1024], 16),
                    in_=kv_out[bass.ds(regs["vb"], NT), :].rearrange("(kt p) d -> p kt d", p=128)),
                    inc="yd", n=16, waits=[ccv])
                xst_ld = {}

                def load_xst(n, dep):
                    qg, dc = divmod(n, DC)
                    i4 = n % 2
                    xst_ld[n] = OP("sync", lambda sy, qg=qg, dc=dc, i4=i4: sy.dma_start(
                        out=xsts[:, i4 * SW:(i4 + 1) * SW],
                        in_=xp_d[dc * 128:(dc + 1) * 128, qg * SW:(qg + 1) * SW]),
                        inc="yd", n=16, waits=[park, dep])

                load_xst(0, phase["a_end"])
                load_xst(1, phase["a_end"])

                def ptslot(qg, kc):
                    return 32 + kc if (qg == 1 and kc < 16) else kc

                ps_free = {0: None, 1: None}
                pav_free = {0: None, 1: None}
                xnw_free = {0: None, 1: None, 2: None, 3: None}
                pl_user = None
                lrow_user = {0: None, 1: None}
                lbc_user = None
                last_t1 = None
                last_store = None
                pre_exps = {0: [], 1: []}

                def emit_scores(qg, kc, extra):
                    p2 = emit_scores.par
                    emit_scores.par ^= 1
                    mm = None
                    for dc in range(DC):
                        mm = OP("tensor", lambda tn, qg=qg, kc=kc, dc=dc, p2=p2: tn.matmul(
                            ps[p2][:, :],
                            kf[:, dc * 2 * NT + kc * 128: dc * 2 * NT + (kc + 1) * 128],
                            qsb[:, (qg % 2) * DC * SW + dc * SW:
                                (qg % 2) * DC * SW + (dc + 1) * SW],
                            start=(dc == 0), stop=(dc == DC - 1)),
                            inc="ts",
                            waits=(([q_lds[qg], ps_free[p2]] + extra) if dc == 0 else []))
                    ex = OP("scalar", lambda sc, qg=qg, kc=kc, p2=p2: sc.activation(
                        pt[:, ptslot(qg, kc) * SW:(ptslot(qg, kc) + 1) * SW],
                        ps[p2][:, :], AF.Exp, scale=1.0 / 32),
                        inc="ss", waits=[mm])
                    ps_free[p2] = ex
                    return mm, ex

                emit_scores.par = 0
                for qg in (0, 1):
                    for kc in range(16):
                        mm, ex = emit_scores(qg, kc, [own_last])
                        pre_exps[qg].append(ex)

                for qg in range(NS):
                    exps = list(pre_exps.get(qg, []))
                    last_sc_mm = None
                    kcs = range(16, 32) if qg < 2 else range(32)
                    for kc in kcs:
                        extra = [kfp_last] if kc == 16 else []
                        mm, ex = emit_scores(qg, kc, extra)
                        exps.append(ex)
                        last_sc_mm = mm
                    if qg + 2 < NS:
                        load_q(qg + 2, [last_sc_mm])
                    lmm = None
                    for kc in range(32):
                        lmm = OP("tensor", lambda tn, qg=qg, kc=kc: tn.matmul(
                            pl[0:1, :], onec_bf[:, :],
                            pt[:, ptslot(qg, kc) * SW:(ptslot(qg, kc) + 1) * SW],
                            start=(kc == 0), stop=(kc == 31)),
                            inc="ts", waits=([exps[-1], pl_user] if kc == 0 else []))
                    linv = OP("vector", lambda v: v.reciprocal(
                        lrow[0:1, 0:SW], pl[0:1, :]),
                        inc="vs", waits=[lmm, lrow_user[0]])
                    lb_mm = OP("tensor", lambda tn: tn.matmul(
                        pl[:, :], ones_f[:, :], lrow[0:1, 0:SW],
                        start=True, stop=True), inc="ts", waits=[linv])
                    lrow_user[0] = lb_mm
                    lb_ev = OP("scalar", lambda sc: sc.copy(lbc[:, :], pl[:, :]),
                               inc="ss", waits=[lb_mm, lbc_user])
                    pl_user = lb_ev
                    for dc in range(DC):
                        n = qg * DC + dc
                        p2 = dc % 2
                        mm = None
                        for kc in range(32):
                            mm = OP("tensor", lambda tn, qg=qg, dc=dc, kc=kc, p2=p2: tn.matmul(
                                pav[p2][:, :],
                                vt[:, kc * 1024 + dc * 128: kc * 1024 + (dc + 1) * 128],
                                pt[:, ptslot(qg, kc) * SW:(ptslot(qg, kc) + 1) * SW],
                                start=(kc == 0), stop=(kc == 31)),
                                inc="ts",
                                waits=([pav_free[p2], lb_ev] if kc == 0
                                       else ([vtp] if kc == 16 else [])))
                        a4, b4 = 0, 1 + n % 2
                        t1 = OP("vector", lambda v, p2=p2, a4=a4: v.tensor_tensor(
                            xnw[:, a4 * SW:(a4 + 1) * SW], pav[p2][:, :], lbc[:, :], ALU.mult),
                            inc="vs", waits=[mm, xnw_free[a4], lb_ev])
                        t2 = OP("vector", lambda v, a4=a4, b4=b4, n=n: v.tensor_tensor(
                            xnw[:, b4 * SW:(b4 + 1) * SW], xnw[:, a4 * SW:(a4 + 1) * SW],
                            xsts[:, (n % 2) * SW:((n % 2) + 1) * SW], ALU.add),
                            inc="vs", waits=[t1, xst_ld[n], xnw_free[b4]])
                        pav_free[p2] = t1
                        last_t1 = t1
                        xnw_free[a4] = t2
                        st = OP("sync", lambda sy, qg=qg, dc=dc, b4=b4: sy.dma_start(
                            out=x2_d[dc * 128:(dc + 1) * 128, qg * SW:(qg + 1) * SW],
                            in_=xnw[:, b4 * SW:(b4 + 1) * SW]),
                            inc="yd", n=16, waits=[t2])
                        xnw_free[b4] = st
                        last_store = st
                        if dc == DC - 1:
                            phase[("x2_qg", qg)] = st
                        phase[("xst_t2", n)] = t2
                        if n + 2 < 32:
                            load_xst(n + 2, t2)
                    lbc_user = last_t1
                phase["x2_done"] = last_store
                phase["av_last"] = mm

        if "ffn2" in PHASES:
            with ExitStack() as sC:
                t = ffn_scope_tensors(sC, "C")
                barrier = phase["av_last"]
                trk2 = {k: barrier for k in (
                    "wgu_user0", "wgu_user1", "wd_user0", "wd_user1",
                    "ws12_user0", "ws12_user1", "pb_user", "pd0_user",
                    "pg0_user", "pg1_user", "pu0_user", "pu1_user")}
                x_lds2 = {}
                for s4 in range(NS):
                    x_lds2[s4] = OP("sync", lambda sy, s4=s4: sy.dma_start(
                        out=t["x"][:, :].rearrange("p (kc n) -> p kc n", kc=DC)[:, :, s4 * SW:(s4 + 1) * SW],
                        in_=r3i(x2_d, DC, NT)[:, :, s4 * SW:(s4 + 1) * SW]),
                        inc="yd", n=16, waits=[phase[("x2_qg", s4)], phase["av_last"]])
                f2 = emit_ffn(t, "C", lambda s: [x_lds2[s]], trk2)
                ystore = OP("sync", lambda sy: sy.dma_start(
                    out=r3i(y_out, DC, NT), in_=r3o(t["x"][:, :], DC)),
                    inc="yd", n=16,
                    waits=[f2[(h2, s2)] for h2 in range(HC) for s2 in range(NS)])
                OP("sync", None, waits=[ystore])
        elif "attn" in PHASES:
            with ExitStack() as sC:
                xt = sC.enter_context(nc.sbuf_tensor("xt_o", [128, DC * NT], F32))
                ld = OP("sync", lambda sy: sy.dma_start(
                    out=r3o(xt[:, :], DC), in_=r3i(x2_d, DC, NT)),
                    inc="yd", n=16, waits=[phase["x2_done"]])
                st = OP("sync", lambda sy: sy.dma_start(
                    out=r3i(y_out, DC, NT), in_=r3o(xt[:, :], DC)),
                    inc="yd", n=16, waits=[ld])
                OP("sync", None, waits=[st])

        ir.finalize(block, sems)
    return nc


def kernel(**inputs):
    f = np.float32
    xfull = np.asarray(inputs["L0_attn_x"], f)
    wq = np.asarray(inputs["L0_attn_wq"], f)
    bq = np.asarray(inputs["L0_attn_bq"], f)
    wk = np.asarray(inputs["L0_attn_wk"], f)
    bk = np.asarray(inputs["L0_attn_bk"], f)
    wv = np.asarray(inputs["L0_attn_wv"], f)
    bv = np.asarray(inputs["L0_attn_bv"], f)
    gw = np.asarray(inputs["L0_ffn_gate_w"], f)

    shared = {
        "wqT": np.ascontiguousarray(wq.T),
        "wkT": np.ascontiguousarray(wk.T),
        "wvT": np.ascontiguousarray(wv.T),
        "bq_pp": np.ascontiguousarray(bq.reshape(DC, 128).T),
        "bk_pp": np.ascontiguousarray(bk.reshape(DC, 128).T),
        "bv_row": np.ascontiguousarray(bv.reshape(1, D)),
        "gwT": np.ascontiguousarray(gw.T),
        "wg_all": np.ascontiguousarray(np.asarray(inputs["L0_ffn_w_gate"], f)),
        "wu_all": np.ascontiguousarray(np.asarray(inputs["L0_ffn_w_up"], f)),
        "wd_all": np.ascontiguousarray(np.asarray(inputs["L0_ffn_w_down"], f)),
        "wsg": np.ascontiguousarray(np.asarray(inputs["L0_ffn_ws_gate"], f)),
        "wsu": np.ascontiguousarray(np.asarray(inputs["L0_ffn_ws_up"], f)),
        "wsd": np.ascontiguousarray(np.asarray(inputs["L0_ffn_ws_down"], f)),
        "ones_col": np.ones((128, 1), f),
        "ones_row": np.ones((1, 128), f),
        "oh": np.kron(np.eye(E, dtype=f), np.ones((1, 128), f)),
        "idn": np.eye(128, dtype=f),
        "cvals": np.tile(np.array(CONST_VALS + [0.0] * (16 - len(CONST_VALS)), f), (128, 1)),
    }
    in_maps = []
    for c in range(NCORE):
        m = dict(shared)
        m["x_in"] = np.ascontiguousarray(xfull[c // 2, (c % 2) * NT:(c % 2 + 1) * NT, :].T)
        in_maps.append(m)

    nc = build_program()
    res = run_bass_kernel_spmd(nc, in_maps, core_ids=list(range(NCORE)))
    global LAST_RES
    LAST_RES = res
    out = np.empty((4, 4096, D), np.float32)
    for c in range(NCORE):
        out[c // 2, (c % 2) * NT:(c % 2 + 1) * NT, :] = res.results[c]["y"].T
    return out
